# version 03
# baseline (speedup 1.0000x reference)
import contextlib
import math

import numpy as np
import ml_dtypes

import concourse.bass as bass
import concourse.mybir as mybir
from concourse.bass_utils import run_bass_kernel_spmd

F32 = mybir.dt.float32
BF16 = mybir.dt.bfloat16
AF = mybir.ActivationFunctionType
ALU = mybir.AluOpType
AX = mybir.AxisListType

NCORES = 8
B_SZ = 2
SEQ = 8192
DM = 4096
TPC = 2048
CHT = 128
NCH = TPC // CHT
NCHH = NCH + 1
TPCH = NCHH * CHT
NHEAD = 32
KVH = 8
HD = 128
GRP = NHEAD // KVH
DKV = KVH * HD
HF = 11008
HCF = HF // 128
W1_SHARD = 1408
W3_SHARD = 512
THETA = 10000.0
EPS = 1e-5
INV_SQRT_HD = 1.0 / math.sqrt(HD)
INV_GN = 1.0 / 16.0

NF4_CODE = np.array([
    -1.0, -0.6961928009986877, -0.5250730514526367, -0.39491748809814453,
    -0.28444138169288635, -0.18477343022823334, -0.09105003625154495, 0.0,
    0.07958029955625534, 0.16093020141124725, 0.24611230194568634,
    0.33791524171829224, 0.44070982933044434, 0.5626170039176941,
    0.7229568362236023, 1.0], dtype=np.float64)
NF4_BOUNDS = [float(x) for x in (NF4_CODE[:-1] + NF4_CODE[1:]) * 0.5]
NF4_DELTA = [float(x) for x in np.diff(NF4_CODE)]

ENGINES = ("tensor", "vector", "scalar", "gpsimd", "sync")


class SemT:
    def __init__(self, h, name):
        self.h = h
        self.name = name
        self.n = 0


class Ins:
    __slots__ = ("i",)

    def __init__(self, i):
        self.i = i

    def inc(self, sem, k=1):
        sem.n += k
        if self.i is not None:
            self.i.then_inc(sem.h, k)
        return self

    def wge(self, sem, val):
        if self.i is not None and val > 0:
            self.i._wait_ge(sem.h, val)
        return self


NOAUTO = {"wait_ge", "sem_inc", "sem_clear", "nop", "dma_start",
          "dma_start_transpose", "collective_compute"}


class Eng:

    def __init__(self, real, name=None, selfsem=None):
        self._e = real
        self._name = name
        self._ss = selfsem

    def __getattr__(self, m):
        e = self._e
        ss = self._ss

        def call(*a, **kw):
            if ss is not None and m not in NOAUTO:
                if e is not None and ss.n > 0:
                    e.wait_ge(ss.h, ss.n)
                ins = getattr(e, m)(*a, **kw) if e is not None else None
                ss.n += 1
                if ins is not None:
                    ins.then_inc(ss.h, 1)
                return Ins(ins)
            if e is None:
                return Ins(None)
            return Ins(getattr(e, m)(*a, **kw))
        return call


class Ctx:
    def __init__(self, sems):
        self.sems = sems
        self.marks = {}
        self.counting = False
        self.engs = {}

    def __getattr__(self, name):
        if name in ENGINES:
            return self.engs[name]
        raise AttributeError(name)

    def mark(self, name, sem):
        if self.counting:
            self.marks[name] = sem.n
        return sem.n

    def at(self, name):
        return self.marks.get(name, 0)


SELFSEM = {"vector": "dv", "scalar": "ac"}


def _mk_engs(ctx, real_map):
    engs = {}
    for e in ENGINES:
        ss = ctx.sems[SELFSEM[e]] if e in SELFSEM else None
        engs[e] = Eng(real_map.get(e), e, ss)
    return engs


def run_phase(ctx, block, emitter):
    start = {k: s.n for k, s in ctx.sems.items()}
    ctx.counting = True
    ctx.engs = _mk_engs(ctx, {})
    emitter(ctx)
    end = {k: s.n for k, s in ctx.sems.items()}
    ctx.counting = False
    for ename in ENGINES:
        for k, s in ctx.sems.items():
            s.n = start[k]

        def body(real_eng, ename=ename):
            ctx.engs = _mk_engs(ctx, {ename: real_eng})
            emitter(ctx)
        getattr(block, ename)(body)
        for k, s in ctx.sems.items():
            assert s.n == end[k], (ename, k, s.n, end[k])


def ebar(c, sems):
    for e in ENGINES:
        eng = getattr(c, e)
        for s_ in sems.values():
            if s_.n > 0:
                eng.wait_ge(s_.h, s_.n)



def build_program(cfg):
    nc = bass.Bass("TRN2", target_bir_lowering=False, debug=False)
    dbg = cfg.get("dbg", set())
    phases = cfg.get("phases", ("deq", "A", "Bpre", "B", "B2", "C"))

    def din(name, shape, dtype=F32):
        return nc.dram_tensor(name, shape, dtype, kind="ExternalInput").ap()

    def dscr(name, shape, dtype, dbgkey=None):
        if dbgkey and dbgkey in dbg:
            return nc.dram_tensor(name, shape, dtype, kind="ExternalOutput").ap()
        return nc.dram_tensor(name, shape, dtype).ap()

    xin = din("xin", [TPCH, DM])
    wq = din("wq", [DM, DM])
    wk = din("wk", [DM, DKV])
    wv = din("wv", [DM, DKV])
    wo = din("wo", [DM, DM])
    w1s = din("w1s", [W1_SHARD, DM])
    w2s = din("w2s", [W1_SHARD, DM])
    w3s = din("w3s", [W3_SHARD, HF])
    cos_t = din("cos_t", [HD, TPCH])
    sin_t = din("sin_t", [HD, TPCH])
    identf = din("identf", [128, 128])
    m_first = din("m_first", [CHT, 2 * CHT])
    m_rest = din("m_rest", [CHT, 2 * CHT])
    tri_m = din("tri_m", [CHT, CHT])
    emask = din("emask", [8, 8])
    pmask_t = din("pmask_t", [128, 8])
    cvec = din("cvec", [128, 8])

    x_out = nc.dram_tensor("x_out", [TPC, DM], F32, kind="ExternalOutput").ap()

    q_sp = dscr("q_sp", [NHEAD, HD, TPCH], BF16, "dbgspill")
    k_sp = dscr("k_sp", [KVH, HD, TPCH], BF16, "dbgspill")
    v_sp = dscr("v_sp", [TPCH, DKV], BF16, "dbgspill")
    o_sp = dscr("o_sp", [NHEAD, HD, TPC], BF16, "dbgo")
    x1_sp = dscr("x1_sp", [TPC, DM], F32, "dbgx1")
    wo_bfd = dscr("wo_bfd", [DM, DM], BF16)
    dq12_loc = dscr("dq12_loc", [2 * DM, W1_SHARD], BF16, "dbgdq")
    dq3_loc = dscr("dq3_loc", [HF, W3_SHARD], BF16, "dbgdq")
    dq12_g = dscr("dq12_g", [NCORES * 2 * DM, W1_SHARD], BF16, "dbgdqg")
    dq3_g = dscr("dq3_g", [NCORES * HF, W3_SHARD], BF16)
    st_loc = dscr("st_loc", [KVH * HD + KVH, HD], F32, "dbgst")
    st_g = dscr("st_g", [NCORES * (KVH * HD + KVH), HD], F32, "dbgst")

    with contextlib.ExitStack() as top:
        E = top.enter_context
        sems = {}
        for nm in ("ldS", "ldG", "pe", "dv", "ac", "cc", "stS",
                   "ldS1", "ldG1", "stS1", "ldP", "ldQ", "ldQ1", "ldV", "ldV1", "ldM"):
            sems[nm] = SemT(E(nc.semaphore("sem_" + nm)), nm)
        ctx = Ctx(sems)
        ldS, ldG = sems["ldS"], sems["ldG"]
        pe, dv, ac = sems["pe"], sems["dv"], sems["ac"]
        cc, stS = sems["cc"], sems["stS"]
        ldSr = [sems["ldS"], sems["ldS1"]]
        ldGr = [sems["ldG"], sems["ldG1"]]
        stSr = [sems["stS"], sems["stS1"]]
        ldP = sems["ldP"]

        block = E(nc.Block())

        ident_bf = E(nc.sbuf_tensor("ident_bf", [128, 128], BF16))
        ident_f = E(nc.sbuf_tensor("ident_f", [128, 128], F32))
        ones_col = E(nc.sbuf_tensor("ones_col", [128, 1], BF16))
        ones_row = E(nc.sbuf_tensor("ones_row", [1, 128], F32))
        cst = E(nc.sbuf_tensor("cst", [128, 8], F32))
        S_f = E(nc.sbuf_tensor("S_f", [128, KVH * 128], F32))
        S_bf = E(nc.sbuf_tensor("S_bf", [128, KVH * 128], BF16))
        aT_acc = E(nc.sbuf_tensor("aT_acc", [128, KVH], F32))

        pre_stack = contextlib.ExitStack()
        wopre = [pre_stack.enter_context(
            nc.sbuf_tensor(f"wopre{i}", [128, DM], BF16)) for i in range(2)]

        def pre(c):
            c.gpsimd.dma_start(out=ident_bf[:, :], in_=identf[:, :]).inc(ldG, 16)
            c.sync.dma_start(out=cst[:, :], in_=cvec[:, :]).inc(ldP, 16)
            (c.sync.dma_start(out=ident_f[:, :], in_=identf[:, :])
             .wge(ldP, ldP.n).inc(ldP, 16))
            for i in range(32):
                wsem = ldGr[i % 2]
                ssem = stSr[i % 2]
                (c.gpsimd.dma_start(
                    out=wopre[i % 2][:, :],
                    in_=wo[i * 128:(i + 1) * 128, :])
                 .wge(ssem, c.at(f"pre_wost_{i - 2}"))
                 .inc(wsem, 16))
                c.mark(f"pre_wold_{i}", wsem)
                (c.sync.dma_start(
                    out=wo_bfd[i * 128:(i + 1) * 128, :],
                    in_=wopre[i % 2][:, :])
                 .wge(wsem, c.at(f"pre_wold_{i}"))
                 .inc(ssem, 16))
                c.mark(f"pre_wost_{i}", ssem)
            c.vector.memset(ones_col[:, :], 1.0)
            c.vector.memset(ones_row[:, :], 1.0)
            c.vector.memset(S_f[:, :], 0.0)
            c.vector.memset(S_bf[:, :], 0.0)
            c.vector.memset(aT_acc[:, :], 0.0)
        run_phase(ctx, block, pre)
        pre_stack.close()

        def emit_deq():
            with contextlib.ExitStack() as sc_:
                Ed = sc_.enter_context
                wt = Ed(nc.sbuf_tensor("dq_wt", [128, HF], F32))
                am = Ed(nc.sbuf_tensor("dq_am", [128, HF // 64], F32))
                rc = Ed(nc.sbuf_tensor("dq_rc", [128, HF // 64], F32))
                nrm = Ed(nc.sbuf_tensor("dq_nrm", [128, HF], F32))
                acc0 = Ed(nc.sbuf_tensor("dq_acc0", [128, HF], BF16))
                acc1 = Ed(nc.sbuf_tensor("dq_acc1", [128, HF], BF16))
                dqv = [Ed(nc.sbuf_tensor(f"dq_dqv{i}", [128, HF], BF16))
                       for i in range(2)]
                tstg = [Ed(nc.sbuf_tensor(f"dq_tstg{i}", [128, 512], BF16))
                        for i in range(2)]
                pst = [Ed(nc.psum_tensor(f"dq_pst{i}", [128, 512], BF16))
                       for i in range(2)]

                jobs = []
                for r in range(W1_SHARD // 128):
                    jobs.append((w1s[r * 128:(r + 1) * 128, :], DM, 0, r))
                for r in range(W1_SHARD // 128):
                    jobs.append((w2s[r * 128:(r + 1) * 128, :], DM, 1, r))
                for r in range(W3_SHARD // 128):
                    jobs.append((w3s[r * 128:(r + 1) * 128, :], HF, 2, r))

                def deq(c):
                    ebar(c, sems)
                    gcnt = 0
                    for i, (src, F, which, r) in enumerate(jobs):
                        nb = F // 64
                        ncks = F // 128
                        (c.sync.dma_start(out=wt[:, :F], in_=src)
                         .wge(dv, c.at(f"dq_nrm_{i - 1}"))
                         .inc(ldS, 16))
                        c.mark(f"dq_ld_{i}", ldS)
                        (c.vector.reduce_max(
                            am[:, :nb],
                            wt[:, :F].rearrange("p (b w) -> p b w", w=64),
                            axis=AX.X, apply_absolute_value=True)
                         .wge(ldS, c.at(f"dq_ld_{i}")))
                        c.vector.tensor_scalar(rc[:, :nb], am[:, :nb],
                                               1e-30, None, ALU.max)
                        c.vector.reciprocal(rc[:, :nb], rc[:, :nb])
                        c.vector.tensor_tensor(
                            nrm[:, :F].rearrange("p (b w) -> p b w", w=64),
                            wt[:, :F].rearrange("p (b w) -> p b w", w=64),
                            rc[:, :nb].rearrange("p (b o) -> p b o", o=1
                                                 ).broadcast_to([128, nb, 64]),
                            ALU.mult)
                        c.mark(f"dq_nrm_{i}", dv)
                        tmp = dqv[i % 2]
                        c.vector.tensor_scalar(acc0[:, :F], nrm[:, :F],
                                               NF4_BOUNDS[0], NF4_DELTA[0],
                                               ALU.is_gt, ALU.mult)
                        c.vector.tensor_scalar(acc1[:, :F], nrm[:, :F],
                                               NF4_BOUNDS[1], NF4_DELTA[1],
                                               ALU.is_gt, ALU.mult)
                        for j in range(2, 15):
                            tgt = acc0 if (j % 2 == 0) else acc1
                            (c.vector.tensor_scalar(tmp[:, :F], nrm[:, :F],
                                                    NF4_BOUNDS[j], NF4_DELTA[j],
                                                    ALU.is_gt, ALU.mult)
                             .wge(pe, c.at(f"dq_tr_{i - 2}") if j == 2 else 0))
                            c.vector.tensor_tensor(tgt[:, :F], tgt[:, :F],
                                                   tmp[:, :F], ALU.add)
                        c.vector.tensor_tensor(acc0[:, :F], acc0[:, :F],
                                               acc1[:, :F], ALU.add)
                        c.vector.scalar_tensor_tensor(
                            tmp[:, :F].rearrange("p (b w) -> p b w", w=64),
                            acc0[:, :F].rearrange("p (b w) -> p b w", w=64),
                            -1.0,
                            am[:, :nb].rearrange("p (b o) -> p b o", o=1
                                                 ).broadcast_to([128, nb, 64]),
                            ALU.add, ALU.mult)
                        c.mark(f"dq_dq_{i}", dv)
                        g4 = (ncks + 3) // 4
                        for g in range(g4):
                            kk = min(4, ncks - g * 4)
                            bank = pst[gcnt % 2]
                            ssem = stSr[gcnt % 2]
                            w = c.at(f"dq_ev_{gcnt - 2}")
                            if w:
                                c.tensor.wait_ge(ac.h, w)
                            ins = None
                            for k in range(kk):
                                cix = g * 4 + k
                                ins = c.tensor.transpose(
                                    bank[:, k * 128:(k + 1) * 128],
                                    tmp[:, cix * 128:(cix + 1) * 128],
                                    ident_bf[:, :])
                                if k == 0:
                                    ins.wge(dv, c.at(f"dq_dq_{i}"))
                                ins.inc(pe)
                            c.mark(f"dq_trp_{gcnt}", pe)
                            if which < 2:
                                o_ap = dq12_loc.rearrange(
                                    "(w dc p) h -> p w dc h", w=2, p=128
                                )[:, which, g * 4:g * 4 + kk,
                                  r * 128:(r + 1) * 128]
                            else:
                                o_ap = dq3_loc.rearrange(
                                    "(dc p) h -> p dc h", p=128
                                )[:, g * 4:g * 4 + kk, r * 128:(r + 1) * 128]
                            w = c.at(f"dq_st_{gcnt - 2}")
                            if w:
                                c.scalar.wait_ge(ssem.h, w)
                            (c.scalar.copy(tstg[gcnt % 2][:, :kk * 128],
                                           bank[:, :kk * 128])
                             .wge(pe, c.at(f"dq_trp_{gcnt}")))
                            c.mark(f"dq_ev_{gcnt}", ac)
                            (c.sync.dma_start(
                                out=o_ap,
                                in_=tstg[gcnt % 2][:, :kk * 128].rearrange(
                                    "p (dc h) -> p dc h", dc=kk))
                             .wge(ac, c.at(f"dq_ev_{gcnt}"))
                             .inc(ssem, 16))
                            c.mark(f"dq_st_{gcnt}", ssem)
                            gcnt += 1
                        c.mark(f"dq_tr_{i}", pe)
                    c.gpsimd.wait_ge(stSr[0].h, stSr[0].n)
                    c.gpsimd.wait_ge(stSr[1].h, stSr[1].n)
                    (c.gpsimd.collective_compute(
                        "AllGather", ALU.bypass,
                        replica_groups=[list(range(NCORES))],
                        ins=[dq12_loc[:, :].opt()],
                        outs=[dq12_g[:, :].opt()])
                     .inc(cc))
                    (c.gpsimd.collective_compute(
                        "AllGather", ALU.bypass,
                        replica_groups=[list(range(NCORES))],
                        ins=[dq3_loc[:, :].opt()],
                        outs=[dq3_g[:, :].opt()])
                     .inc(cc))
                    c.mark("dq_cc", cc)
                run_phase(ctx, block, deq)

        if "deq" in phases and not cfg.get("deq_late"):
            emit_deq()

        if "A" in phases:
            halves = [(0, 9), (9, 8)]
            nhmax = 9
            with contextlib.ExitStack() as sc_:
                Ea = sc_.enter_context
                xch = [Ea(nc.sbuf_tensor(f"a_x{i}", [128, DM], F32))
                       for i in range(2)]
                ln1tm = Ea(nc.sbuf_tensor("a_ln1tm", [128, DM], BF16))
                ln1fm = Ea(nc.sbuf_tensor("a_ln1fm", [128, 32 * nhmax * CHT],
                                          BF16))
                bn6 = Ea(nc.sbuf_tensor("a_bn6", [128, 48], F32))
                mv = Ea(nc.sbuf_tensor("a_mv", [128, 2], F32))
                rstd = Ea(nc.sbuf_tensor("a_rstd", [128, 1], F32))
                wsl = [Ea(nc.sbuf_tensor(f"a_wsl{i}", [128, 32 * 256], BF16))
                       for i in range(2)]
                wvsl = [Ea(nc.sbuf_tensor(f"a_wvsl{i}", [128, DKV], BF16))
                        for i in range(2)]
                stg = [Ea(nc.sbuf_tensor(f"a_stg{i}", [128, 512], BF16))
                       for i in range(2)]

                def phase_a1(c, hi, ch0, nch, psT, psQ, gT, tg_ct):
                    Th = nch * CHT
                    lfm = ln1fm[:, :].rearrange("p (dc t) -> p dc t", t=Th)
                    if True:
                        for j in range(nch):
                            chunk = ch0 + j
                            xb = xch[j % 2]
                            xsem = ldSr[j % 2]
                            (c.sync.dma_start(
                                out=xb[:, :],
                                in_=xin[chunk * CHT:(chunk + 1) * CHT, :])
                             .wge(dv, c.at(f"a_lntm_{hi}_{j - 2}"))
                             .inc(xsem, 16))
                            c.mark(f"a_xld_{hi}_{j}", xsem)
                            for b in range(8):
                                ins = c.vector.bn_stats(
                                    bn6[:, 6 * b:6 * b + 6],
                                    xb[:, 512 * b:512 * (b + 1)])
                                if b == 0:
                                    ins.wge(xsem, c.at(f"a_xld_{hi}_{j}"))
                            c.vector.bn_aggr(mv[:, :], bn6[:, :])
                            c.mark(f"a_bn_{hi}_{j}", dv)
                            (c.scalar.activation(out=rstd[:, :], in_=mv[:, 1:2],
                                                 func=AF.Ln, bias=cst[:, 0:1],
                                                 scale=1.0)
                             .wge(dv, c.at(f"a_bn_{hi}_{j}")))
                            c.scalar.activation(out=rstd[:, :], in_=rstd[:, :],
                                                func=AF.Exp, scale=-0.5)
                            c.mark(f"a_rstd_{hi}_{j}", ac)
                            if j > 0:
                                w = c.at(f"a_trj_{hi}_{j - 1}")
                                if w:
                                    c.vector.wait_ge(pe.h, w)
                            (c.vector.tensor_scalar(
                                ln1tm[:, :], xb[:, :], mv[:, 0:1], rstd[:, :],
                                ALU.subtract, ALU.mult)
                                .wge(ac, c.at(f"a_rstd_{hi}_{j}"))
                                )
                            c.mark(f"a_lntm_{hi}_{j}", dv)
                            for g in range(8):
                                bank = psT[gT[0] % 2]
                                w = c.at(f"a_evT_{hi}_{gT[0] - 2}")
                                if w:
                                    c.tensor.wait_ge(ac.h, w)
                                ins = None
                                for k in range(4):
                                    dcix = g * 4 + k
                                    ins = c.tensor.transpose(
                                        bank[:, k * 128:(k + 1) * 128],
                                        ln1tm[:, dcix * 128:(dcix + 1) * 128],
                                        ident_bf[:, :])
                                    if k == 0:
                                        ins.wge(dv, c.at(f"a_lntm_{hi}_{j}"))
                                    ins.inc(pe)
                                c.mark(f"a_trT_{hi}_{gT[0]}", pe)
                                (c.scalar.copy(
                                    lfm[:, g * 4:g * 4 + 4,
                                        j * CHT:(j + 1) * CHT],
                                    bank[:, :].rearrange("p (k t) -> p k t",
                                                         t=128))
                                 .wge(pe, c.at(f"a_trT_{hi}_{gT[0]}")))
                                c.mark(f"a_evT_{hi}_{gT[0]}", ac)
                                gT[0] += 1
                            c.mark(f"a_trj_{hi}_{j}", pe)
                        c.mark(f"a_lnfmdone_{hi}", ac)
                        if nch == 9:
                            tgs = [(0, 512), (512, 512), (1024, 128)]
                        else:
                            tgs = [(0, 512), (512, 512)]
                        pairs = [(wq, q_sp, f) for f in range(16)] + \
                                [(wk, k_sp, f) for f in range(4)]
                        for pi, (wsrc, dst, f2) in enumerate(pairs):
                            slab = wsl[pi % 2]
                            wsem = ldGr[pi % 2]
                            (c.gpsimd.dma_start(
                                out=slab[:, :].rearrange("p (a f) -> p a f",
                                                         f=256),
                                in_=wsrc[:, f2 * 256:(f2 + 1) * 256].rearrange(
                                    "(a p) f -> p a f", p=128))
                             .wge(pe, c.at(f"a_qkpair_{hi}_{pi - 2}"))
                             .inc(wsem, 16))
                            c.mark(f"a_wld_{hi}_{pi}", wsem)
                            slabv = slab[:, :].rearrange("p (a f) -> p a f",
                                                         f=256)
                            for fc in range(2):
                                fci = f2 * 2 + fc
                                for ti, (t0, tw) in enumerate(tgs):
                                    u = tg_ct[0]
                                    bank = psQ[u % 2]
                                    w = max(c.at(f"a_qkev_{hi}_{u - 2}"),
                                            c.at(f"a_lnfmdone_{hi}")
                                            if (pi == 0 and fc == 0
                                                and ti == 0) else 0)
                                    if w:
                                        c.tensor.wait_ge(ac.h, w)
                                    ins = None
                                    for dc in range(32):
                                        ins = c.tensor.matmul(
                                            bank[:, :tw],
                                            slabv[:, dc,
                                                  fc * 128:(fc + 1) * 128],
                                            lfm[:, dc, t0:t0 + tw],
                                            start=(dc == 0), stop=(dc == 31))
                                        if dc == 0:
                                            ins.wge(wsem,
                                                    c.at(f"a_wld_{hi}_{pi}"))
                                    ins.inc(pe)
                                    c.mark(f"a_qkmm_{hi}_{u}", pe)
                                    sg = stg[u % 2]
                                    w = c.at(f"a_qkst_{hi}_{u - 2}")
                                    if w:
                                        c.scalar.wait_ge(stSr[u % 2].h, w)
                                    (c.scalar.copy(sg[:, :tw], bank[:, :tw])
                                     .wge(pe, c.at(f"a_qkmm_{hi}_{u}")))
                                    c.mark(f"a_qkev_{hi}_{u}", ac)
                                    (c.sync.dma_start(
                                        out=dst[fci, :,
                                                ch0 * CHT + t0:
                                                ch0 * CHT + t0 + tw],
                                        in_=sg[:, :tw])
                                     .wge(ac, c.at(f"a_qkev_{hi}_{u}"))
                                     .inc(stSr[u % 2], 16))
                                    c.mark(f"a_qkst_{hi}_{u}", stSr[u % 2])
                                    tg_ct[0] += 1
                            c.mark(f"a_qkpair_{hi}_{pi}", pe)

                def phase_a2(c, hi, ch0, nch, psV):
                    Th = nch * CHT
                    lfm = ln1fm[:, :].rearrange("p (dc t) -> p dc t", t=Th)
                    if True:
                        vu = [0]
                        ngr = (nch + 3) // 4
                        for gi in range(ngr):
                            tcs = list(range(gi * 4, min(nch, gi * 4 + 4)))
                            for dc in range(32):
                                vdc = gi * 32 + dc
                                slab = wvsl[vdc % 2]
                                vsem = ldGr[vdc % 2]
                                (c.gpsimd.dma_start(
                                    out=slab[:, :],
                                    in_=wv[dc * 128:(dc + 1) * 128, :])
                                 .wge(pe,
                                      c.at(f"a_vmm_{hi}_{vdc - 2}"))
                                 .inc(vsem, 16))
                                c.mark(f"a_vw_{hi}_{vdc}", vsem)
                                if dc == 0:
                                    w = c.at(f"a_vevdone_{hi}_{gi - 1}")
                                    if w:
                                        c.tensor.wait_ge(ac.h, w)
                                ins = None
                                for k, tc in enumerate(tcs):
                                    for vf in range(2):
                                        ins = c.tensor.matmul(
                                            psV[k * 2 + vf][:, :],
                                            lfm[:, dc, tc * CHT:(tc + 1) * CHT],
                                            slab[:, vf * 512:(vf + 1) * 512],
                                            start=(dc == 0), stop=(dc == 31))
                                        if k == 0 and vf == 0:
                                            ins.wge(vsem, c.at(
                                                f"a_vw_{hi}_{vdc}"))
                                ins.inc(pe)
                                c.mark(f"a_vmm_{hi}_{vdc}", pe)
                            for k, tc in enumerate(tcs):
                                for vf in range(2):
                                    u = vu[0]
                                    sg = stg[u % 2]
                                    w = c.at(f"a_vst_{hi}_{u - 2}")
                                    if w:
                                        c.scalar.wait_ge(stSr[u % 2].h, w)
                                    (c.scalar.activation(
                                        out=sg[:, :512],
                                        in_=psV[k * 2 + vf][:, :],
                                        func=AF.Copy, scale=0.5)
                                     .wge(pe, c.at(f"a_vmm_{hi}_{gi * 32 + 31}")))
                                    c.mark(f"a_vev_{hi}_{u}", ac)
                                    (c.sync.dma_start(
                                        out=v_sp[(ch0 + tc) * CHT:
                                                 (ch0 + tc + 1) * CHT,
                                                 vf * 512:(vf + 1) * 512],
                                        in_=sg[:, :512])
                                     .wge(ac, c.at(f"a_vev_{hi}_{u}"))
                                     .inc(stSr[u % 2], 16))
                                    c.mark(f"a_vst_{hi}_{u}", stSr[u % 2])
                                    vu[0] += 1
                            c.mark(f"a_vevdone_{hi}_{gi}", ac)
                for hi, (ch0, nch) in enumerate(halves):
                    with contextlib.ExitStack() as ps_:
                        Ep = ps_.enter_context
                        psT = [Ep(nc.psum_tensor(f"a_psT{hi}_{i}", [128, 512],
                                                 BF16)) for i in range(2)]
                        psQ = [Ep(nc.psum_tensor(f"a_psQ{hi}_{i}", [128, 512],
                                                 F32)) for i in range(2)]

                        def a1(c, hi=hi, ch0=ch0, nch=nch, psT=psT, psQ=psQ):
                            ebar(c, sems)
                            phase_a1(c, hi, ch0, nch, psT, psQ, [0], [0])
                        run_phase(ctx, block, a1)
                    with contextlib.ExitStack() as ps_:
                        Ep = ps_.enter_context
                        psV = [Ep(nc.psum_tensor(f"a_psV{hi}_{i}", [128, 512],
                                                 F32)) for i in range(8)]

                        def a2(c, hi=hi, ch0=ch0, nch=nch, psV=psV):
                            ebar(c, sems)
                            phase_a2(c, hi, ch0, nch, psV)
                        run_phase(ctx, block, a2)

        if "Bpre" in phases or "B" in phases:
            with contextlib.ExitStack() as sc_:
                Eb = sc_.enter_context
                cosb = Eb(nc.sbuf_tensor("b_cos", [128, TPCH], F32))
                sinb = Eb(nc.sbuf_tensor("b_sin", [128, TPCH], F32))
                kc = [Eb(nc.sbuf_tensor(f"b_kc{i}", [128, DKV], BF16))
                      for i in range(2)]
                vc = [Eb(nc.sbuf_tensor(f"b_vc{i}", [128, DKV], BF16))
                      for i in range(2)]
                krr = [Eb(nc.sbuf_tensor(f"b_kr{i}", [128, DKV], BF16))
                       for i in range(2)]
                gsv = Eb(nc.sbuf_tensor("b_gs", [128, DKV], F32))
                Av = Eb(nc.sbuf_tensor("b_A", [128, DKV], F32))
                ekv = Eb(nc.sbuf_tensor("b_ek", [128, DKV], BF16))
                emA = Eb(nc.sbuf_tensor("b_emA", [128, DKV], BF16))
                ktl = Eb(nc.sbuf_tensor("b_kt", [128, DKV], BF16))
                khat = Eb(nc.sbuf_tensor("b_khat", [128, DKV], BF16))
                khat_t = Eb(nc.sbuf_tensor("b_khatt", [128, DKV], BF16))
                eCd = Eb(nc.sbuf_tensor("b_eC", [128, KVH], F32))
                zrow = Eb(nc.sbuf_tensor("b_zrow", [1, DM + DKV], F32))
                zbc = [Eb(nc.sbuf_tensor(f"b_zbc{i}", [128, 512], F32))
                       for i in range(2)]
                tmpk = Eb(nc.sbuf_tensor("b_tmpk", [128, DKV], F32))
                mf_sb = Eb(nc.sbuf_tensor("b_mf", [128, 256], BF16))
                mr_sb = Eb(nc.sbuf_tensor("b_mr", [128, 256], BF16))
                tri_sb = Eb(nc.sbuf_tensor("b_tri", [128, 128], BF16))

                ldQr = [sems["ldQ"], sems["ldQ1"]]

                def kv_load(c, pref, i, kwaitD, kwaitA, vwait):
                    ksem = ldSr[i % 2]
                    vsem = [sems["ldV"], sems["ldV1"]][i % 2]
                    tr = slice(i * CHT, (i + 1) * CHT)
                    if kwaitA:
                        c.sync.wait_ge(ac.h, kwaitA)
                    (c.sync.dma_start(
                        out=kc[i % 2][:, :].rearrange("p (g t) -> p g t",
                                                      t=CHT),
                        in_=k_sp[:, :, tr].rearrange("g h t -> h g t"))
                     .wge(dv, kwaitD).inc(ksem, 16))
                    c.mark(f"{pref}_kld_{i}", ksem)
                    (c.sync.dma_start(out=vc[i % 2][:, :], in_=v_sp[tr, :])
                     .wge(pe, vwait).inc(vsem, 16))
                    c.mark(f"{pref}_vld_{i}", vsem)

                def k_rope(c, pref, i):
                    kci = kc[i % 2][:, :]
                    tr = slice(i * CHT, (i + 1) * CHT)
                    (c.vector.tensor_scalar(tmpk[0:64, :], kci[64:128, :],
                                            -1.0, None, ALU.mult)
                     .wge(ldSr[i % 2], c.at(f"{pref}_kld_{i}")))
                    c.vector.tensor_scalar(tmpk[64:128, :], kci[0:64, :],
                                           1.0, None, ALU.mult)
                    k3 = kci.rearrange("p (g t) -> p g t", t=CHT)
                    t3 = tmpk[:, :].rearrange("p (g t) -> p g t", t=CHT)
                    cs3 = cosb[:, tr].rearrange("p (o t) -> p o t", o=1
                                                ).broadcast_to([128, KVH, CHT])
                    sn3 = sinb[:, tr].rearrange("p (o t) -> p o t", o=1
                                                ).broadcast_to([128, KVH, CHT])
                    kr3 = krr[i % 2][:, :].rearrange("p (g t) -> p g t", t=CHT)
                    c.vector.tensor_tensor(t3, t3, sn3, ALU.mult)
                    c.vector.tensor_tensor(kr3, k3, cs3, ALU.mult)
                    c.vector.tensor_tensor(kr3, kr3, t3, ALU.add)
                    c.mark(f"{pref}_krope_{i}", dv)

                def k_gate(c, pref, i):
                    kci = kc[i % 2][:, :]
                    w = c.at(f"{pref}_A_{i - 1}")
                    if w:
                        c.scalar.wait_ge(dv.h, w)
                    (c.scalar.activation(out=gsv[:, :], in_=kci, func=AF.Abs)
                     .wge(ldSr[i % 2], c.at(f"{pref}_kld_{i}")))
                    c.scalar.activation(out=gsv[:, :], in_=gsv[:, :],
                                        func=AF.Exp, scale=-1.0)
                    c.scalar.activation(out=gsv[:, :], in_=gsv[:, :],
                                        func=AF.Ln, bias=cst[:, 1:2])
                    c.mark(f"{pref}_gsln_{i}", ac)
                    (c.vector.tensor_scalar(tmpk[:, :], kci, 0.0, None,
                                            ALU.min)
                     .wge(ldSr[i % 2], c.at(f"{pref}_kld_{i}")))
                    (c.vector.tensor_tensor(gsv[:, :], tmpk[:, :], gsv[:, :],
                                            ALU.subtract)
                     .wge(ac, c.at(f"{pref}_gsln_{i}")))
                    c.mark(f"{pref}_kfreeD_{i}", dv)
                    for g in range(KVH):
                        sl = slice(g * CHT, (g + 1) * CHT)
                        c.vector.tensor_tensor_scan(
                            Av[:, sl], gsv[:, sl], gsv[:, sl], 0.0,
                            ALU.add, ALU.bypass)
                    lastc = Av[:, :].rearrange("p (g t) -> p g t", t=CHT
                                               )[:, :, CHT - 1:CHT]
                    c.vector.tensor_tensor(
                        aT_acc[:, :].rearrange("p (g o) -> p g o", o=1),
                        aT_acc[:, :].rearrange("p (g o) -> p g o", o=1),
                        lastc, ALU.add)
                    c.mark(f"{pref}_A_{i}", dv)
                    (c.scalar.activation(
                        out=eCd[:, :].rearrange("p (g o) -> p g o", o=1),
                        in_=lastc, func=AF.Exp, scale=INV_GN)
                     .wge(dv, c.at(f"{pref}_A_{i}")))
                    c.scalar.activation(out=ekv[:, :], in_=kci, func=AF.Exp)
                    c.mark(f"{pref}_kfreeA_{i}", ac)
                    c.scalar.activation(out=emA[:, :], in_=Av[:, :],
                                        func=AF.Exp, scale=-INV_GN)
                    c.mark(f"{pref}_ekv_{i}", ac)

                def z_mm_unit(c, zc, srcs, dst_off, wname, psZmm):
                    n = zc[0]
                    w = c.at(f"z_ev_{n - 1}")
                    if w:
                        c.tensor.wait_ge(dv.h, w)
                    insm = None
                    for j, src in enumerate(srcs):
                        insm = c.tensor.matmul(
                            psZmm[0:1, j * 128:(j + 1) * 128],
                            ones_col[:, :], src, start=True, stop=True)
                        insm.wge(ac, c.at(wname))
                        insm.inc(pe)
                    c.mark(f"z_mm_{n}", pe)
                    (c.vector.tensor_copy(
                        zrow[0:1, dst_off:dst_off + len(srcs) * 128],
                        psZmm[0:1, 0:len(srcs) * 128])
                     .wge(pe, c.at(f"z_mm_{n}")))
                    c.mark(f"z_ev_{n}", dv)
                    zc[0] += 1

                def z_bc_unit(c, zb, src_off, width, recip_mark, psZbc):
                    m = zb[0]
                    w = c.at(f"z_bcev_{m - 1}")
                    if w:
                        c.tensor.wait_ge(ac.h, w)
                    insm = c.tensor.matmul(
                        psZbc[:, 0:width], ones_row[0:1, :],
                        zrow[0:1, src_off:src_off + width],
                        start=True, stop=True)
                    insm.wge(dv, c.at(recip_mark))
                    insm.inc(pe)
                    c.mark(f"z_bc_{m}", pe)
                    w = c.at(f"z_bcused_{m - 2}")
                    if w:
                        c.scalar.wait_ge(dv.h, w)
                    (c.scalar.copy(zbc[m % 2][:, 0:width], psZbc[:, 0:width])
                     .wge(pe, c.at(f"z_bc_{m}")))
                    c.mark(f"z_bcev_{m}", ac)
                    zb[0] += 1
                    return m

                def khat_make(c, pref, i, zb, recip_mark, psZbc):
                    (c.vector.tensor_tensor(ktl[:, :], ekv[:, :], emA[:, :],
                                            ALU.mult)
                     .wge(ac, c.at(f"{pref}_ekv_{i}")))
                    for half in range(2):
                        m = z_bc_unit(c, zb, DM + half * 512, 512,
                                      f"{pref}_zrecip_{i}", psZbc)
                        (c.vector.tensor_tensor(
                            ktl[:, half * 512:(half + 1) * 512],
                            ktl[:, half * 512:(half + 1) * 512],
                            zbc[m % 2][:, 0:512], ALU.mult)
                         .wge(ac, c.at(f"z_bcev_{m}")))
                        c.mark(f"z_bcused_{m}", dv)
                    c.mark(f"{pref}_kt_{i}", dv)
                    for g in range(KVH):
                        sl = slice(g * CHT, (g + 1) * CHT)
                        c.vector.tensor_scalar(khat[:, sl], ktl[:, sl],
                                               eCd[:, g:g + 1], None,
                                               ALU.mult)
                    c.mark(f"{pref}_khat_{i}", dv)

                def khat_transpose(c, pref, i, gTc, psTb):
                    for g in range(KVH):
                        n = gTc[0]
                        base = 256 + (n % 2) * 128
                        w = c.at(f"kh_ev_{n - 2}")
                        if w:
                            c.tensor.wait_ge(dv.h, w)
                        ins = c.tensor.transpose(
                            psTb[:, base:base + 128],
                            khat[:, g * 128:(g + 1) * 128], ident_bf[:, :])
                        ins.wge(dv, c.at(f"{pref}_khat_{i}"))
                        ins.inc(pe)
                        c.mark(f"kh_tr_{n}", pe)
                        (c.vector.tensor_copy(
                            khat_t[:, g * 128:(g + 1) * 128],
                            psTb[:, base:base + 128])
                         .wge(pe, c.at(f"kh_tr_{n}")))
                        c.mark(f"kh_ev_{n}", dv)
                        gTc[0] += 1
                    c.mark(f"{pref}_khatt_{i}", dv)

                def state_update(c, pref, i, ps_a, ps_b, extra_pe_wait=0,
                                 extra_ac_wait=0):
                    w = c.at(f"{pref}_spsfree_{i - 1}")
                    if w:
                        c.tensor.wait_ge(dv.h, w)
                    if extra_ac_wait:
                        c.tensor.wait_ge(ac.h, extra_ac_wait)
                    vsem = [sems["ldV"], sems["ldV1"]][i % 2]
                    c.tensor.wait_ge(vsem.h, c.at(f"{pref}_vld_{i}"))
                    insm = None
                    for g in range(KVH):
                        bank = ps_a if g < 4 else ps_b
                        bsl = slice((g % 4) * 128, (g % 4 + 1) * 128)
                        sl = slice(g * CHT, (g + 1) * CHT)
                        insm = c.tensor.matmul(
                            bank[:, bsl], khat_t[:, sl], vc[i % 2][:, sl],
                            start=True, stop=True)
                        if g == 0:
                            insm.wge(dv, c.at(f"{pref}_khatt_{i}"))
                        insm.inc(pe)
                    c.mark(f"{pref}_smm_{i}", pe)
                    for g in range(KVH):
                        bank = ps_a if g < 4 else ps_b
                        bsl = slice((g % 4) * 128, (g % 4 + 1) * 128)
                        sl = slice(g * CHT, (g + 1) * CHT)
                        ins = c.vector.scalar_tensor_tensor(
                            S_f[:, sl], S_f[:, sl], eCd[:, g:g + 1],
                            bank[:, bsl], ALU.mult, ALU.add)
                        if g == 0:
                            ins.wge(pe, c.at(f"{pref}_smm_{i}"))
                    c.mark(f"{pref}_spsfree_{i}", dv)
                    c.vector.tensor_copy(S_bf[:, :], S_f[:, :])
                    c.mark(f"{pref}_sbf_{i}", dv)

                if "Bpre" in phases:
                    with contextlib.ExitStack() as ps_:
                        Ep = ps_.enter_context
                        psZmm = Ep(nc.psum_tensor("bp_psZm", [128, 512], F32))
                        psZbc = Ep(nc.psum_tensor("bp_psZb", [128, 512], F32))
                        psTb = Ep(nc.psum_tensor("bp_psT", [128, 512], BF16))
                        psSa = Ep(nc.psum_tensor("bp_psSa", [128, 512], F32))
                        psSb = Ep(nc.psum_tensor("bp_psSb", [128, 512], F32))

                        def bpre(c):
                            ebar(c, sems)
                            zc, zb, gTc = [0], [0], [0]
                            for i in range(1, NCHH):
                                kv_load(c, "bp", i,
                                        c.at(f"bp_kfreeD_{i - 2}"),
                                        c.at(f"bp_kfreeA_{i - 2}"),
                                        c.at(f"bp_smm_{i - 2}"))
                                k_gate(c, "bp", i)
                                for half in range(2):
                                    z_mm_unit(
                                        c, zc,
                                        [ekv[:, (half * 4 + g) * 128:
                                             (half * 4 + g + 1) * 128]
                                         for g in range(4)],
                                        DM + half * 512,
                                        f"bp_ekv_{i}", psZmm)
                                (c.vector.reciprocal(zrow[0:1, DM:DM + DKV],
                                                     zrow[0:1, DM:DM + DKV]))
                                c.mark(f"bp_zrecip_{i}", dv)
                                khat_make(c, "bp", i, zb,
                                          f"bp_zrecip_{i}", psZbc)
                                khat_transpose(c, "bp", i, gTc, psTb)
                                state_update(c, "bp", i, psSa, psSb)
                            for g in range(KVH):
                                (c.sync.dma_start(
                                    out=st_loc[g * 128:(g + 1) * 128, :],
                                    in_=S_f[:, g * 128:(g + 1) * 128])
                                 .wge(dv, c.at(f"bp_spsfree_{NCHH - 1}"))
                                 .inc(stSr[g % 2], 16))
                            c.mark("bp_sst0", stSr[0])
                            c.mark("bp_sst1", stSr[1])
                            w = c.at(f"bp_spsfree_{NCHH - 1}")
                            c.tensor.wait_ge(dv.h, w)
                            ins = c.tensor.transpose(
                                psSa[0:KVH, 0:128], aT_acc[:, :],
                                ident_f[:, :])
                            ins.inc(pe)
                            c.mark("bp_att", pe)
                            (c.vector.tensor_copy(tmpk[0:KVH, 0:128],
                                                  psSa[0:KVH, 0:128])
                             .wge(pe, c.at("bp_att")))
                            c.mark("bp_attev", dv)
                            (c.sync.dma_start(out=st_loc[KVH * HD:, :],
                                              in_=tmpk[0:KVH, 0:128])
                             .wge(dv, c.at("bp_attev"))
                             .inc(stSr[0], 16))
                            c.gpsimd.wait_ge(stSr[0].h, stSr[0].n)
                            c.gpsimd.wait_ge(stSr[1].h, stSr[1].n)
                            (c.gpsimd.collective_compute(
                                "AllGather", ALU.bypass,
                                replica_groups=[list(range(NCORES))],
                                ins=[st_loc[:, :].opt()],
                                outs=[st_g[:, :].opt()])
                             .inc(cc))
                            c.mark("bp_cc", cc)
                        run_phase(ctx, block, bpre)

                    with contextlib.ExitStack() as ps_:
                        Ep = ps_.enter_context
                        psW = Ep(nc.psum_tensor("cb_psW", [128, 512], F32))
                        aTA = Ep(nc.sbuf_tensor("cb_aTA", [8, DKV], F32))
                        em_sb = Ep(nc.sbuf_tensor("cb_em", [8, 8], F32))
                        pm_sb = Ep(nc.sbuf_tensor("cb_pm", [128, 8], F32))
                        wv_sb = Ep(nc.sbuf_tensor("cb_wv", [128, 64], F32))
                        sjb = [Ep(nc.sbuf_tensor(f"cb_sj{i}", [128, DKV],
                                                 F32)) for i in range(2)]

                        def comb(c):
                            ebar(c, sems)
                            c.sync.wait_ge(cc.h, cc.n)
                            rowsz = KVH * HD + KVH
                            c.sync.dma_start(out=em_sb[:, :],
                                             in_=emask[:, :]).inc(ldP, 16)
                            c.sync.dma_start(out=pm_sb[:, :],
                                             in_=pmask_t[:, :]).inc(ldP, 16)
                            for r in range(NCORES):
                                (c.sync.dma_start(
                                    out=aTA[r:r + 1, :],
                                    in_=st_g[r * rowsz + KVH * HD:
                                             (r + 1) * rowsz, :].rearrange(
                                        "(o a) b -> o (a b)", o=1))
                                 .inc(ldP, 16))
                            c.mark("cb_ld", ldP)
                            insm = None
                            for g in range(KVH):
                                insm = c.tensor.matmul(
                                    psW[:, g * 8:(g + 1) * 8],
                                    aTA[:, g * 128:(g + 1) * 128],
                                    em_sb[:, :], start=True, stop=True)
                                insm.wge(ldP, c.at("cb_ld"))
                                insm.inc(pe)
                            c.mark("cb_wmm", pe)
                            (c.scalar.activation(out=wv_sb[:, :],
                                                 in_=psW[:, 0:64],
                                                 func=AF.Exp, scale=INV_GN)
                             .wge(pe, c.at("cb_wmm")))
                            c.mark("cb_wexp", ac)
                            (c.vector.tensor_tensor(
                                wv_sb[:, :].rearrange("p (g j) -> p g j", j=8),
                                wv_sb[:, :].rearrange("p (g j) -> p g j", j=8),
                                pm_sb[:, :].rearrange("p (o j) -> p o j", o=1
                                                      ).broadcast_to(
                                    [128, KVH, 8]), ALU.mult)
                             .wge(ac, c.at("cb_wexp")))
                            c.vector.memset(S_f[:, :], 0.0)
                            c.mark("cb_wv", dv)
                            for j in range(NCORES):
                                (c.sync.dma_start(
                                    out=sjb[j % 2][:, :].rearrange(
                                        "p (g d) -> p g d", d=HD),
                                    in_=st_g[j * rowsz:j * rowsz + KVH * HD,
                                             :].rearrange(
                                        "(g p) d -> p g d", p=128))
                                 .wge(dv, c.at(f"cb_sjfree_{j - 2}"))
                                 .inc(ldSr[j % 2], 16))
                                c.mark(f"cb_sjld_{j}", ldSr[j % 2])
                                for g in range(KVH):
                                    sl = slice(g * 128, (g + 1) * 128)
                                    ins = c.vector.scalar_tensor_tensor(
                                        S_f[:, sl], sjb[j % 2][:, sl],
                                        wv_sb[:, g * 8 + j:g * 8 + j + 1],
                                        S_f[:, sl], ALU.mult, ALU.add)
                                    if g == 0:
                                        ins.wge(ldSr[j % 2],
                                                c.at(f"cb_sjld_{j}"))
                                c.mark(f"cb_sjfree_{j}", dv)
                            c.vector.tensor_copy(S_bf[:, :], S_f[:, :])
                            c.mark("cb_done", dv)
                        run_phase(ctx, block, comb)

                if "B" in phases:
                    with contextlib.ExitStack() as ps_:
                        Ep = ps_.enter_context
                        qc = [Ep(nc.sbuf_tensor(f"b_qc{i}", [128, DM], BF16))
                              for i in range(2)]
                        qa = Ep(nc.sbuf_tensor("b_qa", [128, DM], F32))
                        eqv = Ep(nc.sbuf_tensor("b_eq", [128, DM], BF16))
                        eql = Ep(nc.sbuf_tensor("b_eql", [128, DM], BF16))
                        qtl = Ep(nc.sbuf_tensor("b_qt", [128, DM], BF16))
                        qro = Ep(nc.sbuf_tensor("b_qro", [128, DM], BF16))
                        tmpq = Ep(nc.sbuf_tensor("b_tmpq", [128, DM], F32))
                        ptg = Ep(nc.sbuf_tensor("b_ptg", [128, 512], BF16))
                        pts = Ep(nc.sbuf_tensor("b_pts", [128, 512], BF16))
                        prow = Ep(nc.sbuf_tensor("b_prow", [128, 256], BF16))
                        mx = Ep(nc.sbuf_tensor("b_mx", [128, 4], F32))
                        ost = [Ep(nc.sbuf_tensor(f"b_ost{i}", [128, DM],
                                                 BF16)) for i in range(2)]
                        psc = Ep(nc.psum_tensor("bm_psc", [128, 512], F32))
                        psPT = [Ep(nc.psum_tensor(f"bm_psPT{i}", [128, 512],
                                                  F32)) for i in range(2)]
                        psO = [Ep(nc.psum_tensor(f"bm_psO{i}", [128, 512],
                                                 F32)) for i in range(2)]
                        psZmm = Ep(nc.psum_tensor("bm_psZm", [128, 512], F32))
                        psZbc = Ep(nc.psum_tensor("bm_psZb", [128, 512], F32))
                        psTb = Ep(nc.psum_tensor("bm_psT", [128, 512], BF16))

                        def swa_head(c, i, g, hh, zb_unused):
                            h = g * 4 + hh
                            half = hh % 2
                            w = c.at(f"bm_pexp_{i}_{h - 2}")
                            if w:
                                c.tensor.wait_ge(ac.h, w)
                            for kk in range(2):
                                krx = krr[(i - 1 + kk) % 2]
                                insm = c.tensor.matmul(
                                    psc[:, half * 256 + kk * 128:
                                        half * 256 + (kk + 1) * 128],
                                    qro[:, h * 128:(h + 1) * 128],
                                    krx[:, g * 128:(g + 1) * 128],
                                    start=True, stop=True)
                                if kk == 0:
                                    insm.wge(dv, c.at(f"bm_qrope_{i}"))
                                insm.inc(pe)
                            c.mark(f"bm_scmm_{i}_{h}", pe)
                            scs = psc[:, half * 256:(half + 1) * 256]
                            (c.vector.reduce_max(mx[:, half:half + 1], scs,
                                                 axis=AX.X, negate=True)
                             .wge(pe, c.at(f"bm_scmm_{i}_{h}")))
                            c.vector.tensor_scalar(mx[:, half:half + 1],
                                                   mx[:, half:half + 1],
                                                   INV_SQRT_HD, None,
                                                   ALU.mult)
                            c.mark(f"bm_mx_{i}_{h}", dv)
                            mask = mf_sb if i == 1 else mr_sb
                            (c.scalar.activation(
                                out=prow[:, :], in_=scs, func=AF.Exp,
                                scale=INV_SQRT_HD,
                                bias=mx[:, half:half + 1])
                             .wge(dv, c.at(f"bm_mx_{i}_{h}")))
                            c.mark(f"bm_pexp_{i}_{h}", ac)
                            (c.vector.tensor_tensor(prow[:, :], prow[:, :],
                                                    mask[:, :], ALU.mult)
                             .wge(ac, c.at(f"bm_pexp_{i}_{h}")))
                            c.vector.reduce_sum(mx[:, 2:3], prow[:, :],
                                                axis=AX.X)
                            c.vector.reciprocal(mx[:, 2:3], mx[:, 2:3])
                            c.vector.tensor_scalar(prow[:, :], prow[:, :],
                                                   mx[:, 2:3], None, ALU.mult)
                            c.mark(f"bm_pn_{i}_{h}", dv)
                            w = c.at(f"bm_ptsu_{i}_{h - 2}")
                            if w:
                                c.tensor.wait_ge(pe.h, w)
                            for kk in range(2):
                                ins = c.tensor.transpose(
                                    psTb[:, kk * 128:(kk + 1) * 128],
                                    prow[:, kk * 128:(kk + 1) * 128],
                                    ident_bf[:, :])
                                ins.wge(dv, c.at(f"bm_pn_{i}_{h}"))
                                ins.inc(pe)
                            c.mark(f"bm_ptr_{i}_{h}", pe)
                            w = c.at(f"bm_omm_{i}_{h - 2}")
                            if w:
                                c.vector.wait_ge(pe.h, w)
                            (c.vector.tensor_copy(
                                pts[:, half * 256:(half + 1) * 256],
                                psTb[:, 0:256])
                             .wge(pe, c.at(f"bm_ptr_{i}_{h}")))
                            c.mark(f"bm_pts_{i}_{h}", dv)
                            bank = psO[g % 2]
                            ob = bank[:, hh * 128:(hh + 1) * 128]
                            vprev = vc[(i - 1) % 2]
                            vcur = vc[i % 2]
                            sl = slice(g * 128, (g + 1) * 128)
                            if hh == 0:
                                w = c.at(f"bm_oevg_{i}_{g - 2}")
                                if g <= 1:
                                    w = max(w, c.at(f"bm_oevg_{i - 1}_{g + 6}"))
                                if w:
                                    c.tensor.wait_ge(ac.h, w)
                                if g <= 1:
                                    w2 = c.at(f"bm_spsfree_{i - 1}")
                                    if w2:
                                        c.tensor.wait_ge(dv.h, w2)
                                if g == 0:
                                    vsp = [sems["ldV"], sems["ldV1"]]
                                    c.tensor.wait_ge(
                                        vsp[(i - 1) % 2].h,
                                        c.at(f"bm_vld_{i - 1}"))
                                    c.tensor.wait_ge(
                                        vsp[i % 2].h, c.at(f"bm_vld_{i}"))
                            m1 = c.tensor.matmul(
                                ob, vprev[:, sl],
                                pts[:, half * 256:half * 256 + 128],
                                start=True, stop=False)
                            m1.wge(dv, c.at(f"bm_pts_{i}_{h}"))
                            c.tensor.matmul(
                                ob, vcur[:, sl],
                                pts[:, half * 256 + 128:half * 256 + 256],
                                start=False, stop=False)
                            m3 = c.tensor.matmul(
                                ob, vcur[:, sl],
                                ptg[:, hh * 128:(hh + 1) * 128],
                                start=False, stop=False)
                            m3.wge(dv, c.at(f"bm_ptg_{i}_{g}"))
                            m4 = c.tensor.matmul(
                                ob, S_bf[:, sl],
                                qtl[:, h * 128:(h + 1) * 128],
                                start=False, stop=True)
                            m4.wge(dv, c.at(f"bm_sbf_{i - 1}")
                                   if i > 1 else c.at("cb_done"))
                            m4.inc(pe)
                            c.mark(f"bm_omm_{i}_{h}", pe)
                            c.mark(f"bm_ptsu_{i}_{h}", pe)

                        def bmain(c):
                            ebar(c, sems)
                            c.sync.dma_start(out=cosb[:, :],
                                             in_=cos_t[:, :]).inc(ldP, 16)
                            c.sync.dma_start(out=sinb[:, :],
                                             in_=sin_t[:, :]).inc(ldP, 16)
                            ldM = sems["ldM"]
                            (c.gpsimd.dma_start(out=mf_sb[:, :],
                                                 in_=m_first[:, :])
                             .wge(ldM, ldM.n).inc(ldM, 16))
                            (c.gpsimd.dma_start(out=mr_sb[:, :],
                                                in_=m_rest[:, :])
                             .wge(ldM, ldM.n).inc(ldM, 16))
                            (c.gpsimd.dma_start(out=tri_sb[:, :],
                                                in_=tri_m[:, :])
                             .wge(ldM, ldM.n).inc(ldM, 16))
                            c.mark("bm_tbl", ldM)
                            for e in ENGINES:
                                getattr(c, e).wait_ge(ldP.h, ldP.n)
                                getattr(c, e).wait_ge(ldM.h, ldM.n)
                            zc, zb, gTc = [0], [0], [0]
                            kv_load(c, "bm", 0, 0, 0, 0)
                            k_rope(c, "bm", 0)
                            for i in range(1, NCHH):
                                ksem = ldSr[i % 2]
                                vsem = ldGr[i % 2]
                                qsem = ldQr[i % 2]
                                vwait = c.at(f"bm_omm_{i - 1}_{NHEAD - 1}")
                                kv_load(c, "bm", i,
                                        c.at(f"bm_kfreeD_{i - 2}"),
                                        c.at(f"bm_kfreeA_{i - 2}"), vwait)
                                tr = slice(i * CHT, (i + 1) * CHT)
                                if c.at(f"bm_qfreeA_{i - 2}"):
                                    c.sync.wait_ge(ac.h,
                                                   c.at(f"bm_qfreeA_{i - 2}"))
                                (c.sync.dma_start(
                                    out=qc[i % 2][:, :].rearrange(
                                        "p (h t) -> p h t", t=CHT),
                                    in_=q_sp[:, :, tr].rearrange(
                                        "h d t -> d h t"))
                                 .wge(dv, c.at(f"bm_qfreeD_{i - 2}"))
                                 .inc(qsem, 16))
                                c.mark(f"bm_qld_{i}", qsem)
                                k_rope(c, "bm", i)
                                k_gate(c, "bm", i)
                                qci = qc[i % 2][:, :]
                                (c.vector.tensor_scalar(
                                    tmpq[0:64, :], qci[64:128, :], -1.0,
                                    None, ALU.mult)
                                 .wge(qsem, c.at(f"bm_qld_{i}")))
                                c.vector.tensor_scalar(tmpq[64:128, :],
                                                       qci[0:64, :], 1.0,
                                                       None, ALU.mult)
                                q3 = qci.rearrange("p (h t) -> p h t", t=CHT)
                                t3 = tmpq[:, :].rearrange("p (h t) -> p h t",
                                                          t=CHT)
                                cs3 = cosb[:, tr].rearrange(
                                    "p (o t) -> p o t", o=1).broadcast_to(
                                    [128, NHEAD, CHT])
                                sn3 = sinb[:, tr].rearrange(
                                    "p (o t) -> p o t", o=1).broadcast_to(
                                    [128, NHEAD, CHT])
                                qr3 = qro[:, :].rearrange("p (h t) -> p h t",
                                                          t=CHT)
                                c.vector.tensor_tensor(t3, t3, sn3, ALU.mult)
                                c.vector.tensor_tensor(qr3, q3, cs3, ALU.mult)
                                c.vector.tensor_tensor(qr3, qr3, t3, ALU.add)
                                c.mark(f"bm_qrope_{i}", dv)
                                wA = c.at(f"bm_A_{i}")
                                c.vector.wait_ge(dv.h, wA)
                                for h in range(NHEAD):
                                    g = h // GRP
                                    c.vector.scalar_tensor_tensor(
                                        qa[:, h * 128:(h + 1) * 128],
                                        Av[:, g * 128:(g + 1) * 128],
                                        INV_GN,
                                        qci[:, h * 128:(h + 1) * 128],
                                        ALU.mult, ALU.add)
                                c.mark(f"bm_qfreeD_{i}", dv)
                                (c.scalar.activation(out=eql[:, :],
                                                     in_=qa[:, :],
                                                     func=AF.Exp)
                                 .wge(dv, c.at(f"bm_qfreeD_{i}")))
                                c.scalar.activation(out=eqv[:, :], in_=qci,
                                                    func=AF.Exp)
                                c.mark(f"bm_qfreeA_{i}", ac)
                                c.mark(f"bm_eq_{i}", ac)
                                for half in range(2):
                                    z_mm_unit(
                                        c, zc,
                                        [ekv[:, (half * 4 + g) * 128:
                                             (half * 4 + g + 1) * 128]
                                         for g in range(4)],
                                        DM + half * 512,
                                        f"bm_ekv_{i}", psZmm)
                                for r8 in range(8):
                                    z_mm_unit(
                                        c, zc,
                                        [eqv[:, (r8 * 4 + j) * 128:
                                             (r8 * 4 + j + 1) * 128]
                                         for j in range(4)],
                                        r8 * 512, f"bm_eq_{i}", psZmm)
                                c.vector.reciprocal(zrow[0:1, :],
                                                    zrow[0:1, :])
                                c.mark(f"bm_zrecip_{i}", dv)
                                khat_make(c, "bm", i, zb,
                                          f"bm_zrecip_{i}", psZbc)
                                for r8 in range(8):
                                    m = z_bc_unit(c, zb, r8 * 512, 512,
                                                  f"bm_zrecip_{i}", psZbc)
                                    (c.vector.tensor_tensor(
                                        qtl[:, r8 * 512:(r8 + 1) * 512],
                                        eql[:, r8 * 512:(r8 + 1) * 512],
                                        zbc[m % 2][:, 0:512], ALU.mult)
                                     .wge(ac, c.at(f"z_bcev_{m}")))
                                    c.mark(f"z_bcused_{m}", dv)
                                c.mark(f"bm_qtldone_{i}", dv)
                                for g in range(KVH):
                                    bankPT = psPT[g % 2]
                                    w = c.at(f"bm_ptgused_{i}_{g - 2}")
                                    if w:
                                        c.tensor.wait_ge(dv.h, w)
                                    insm = None
                                    for hh in range(4):
                                        h = g * 4 + hh
                                        insm = c.tensor.matmul(
                                            bankPT[:, hh * 128:(hh + 1) * 128],
                                            ktl[:, g * 128:(g + 1) * 128],
                                            qtl[:, h * 128:(h + 1) * 128],
                                            start=True, stop=True)
                                        if hh == 0:
                                            insm.wge(dv,
                                                     c.at(f"bm_qtldone_{i}"))
                                        insm.inc(pe)
                                    c.mark(f"bm_ptgmm_{i}_{g}", pe)
                                    (c.vector.tensor_tensor(
                                        ptg[:, :].rearrange(
                                            "p (h t) -> p h t", t=CHT),
                                        bankPT[:, :].rearrange(
                                            "p (h t) -> p h t", t=CHT),
                                        tri_sb[:, :].rearrange(
                                            "p (o t) -> p o t",
                                            o=1).broadcast_to([128, 4, CHT]),
                                        ALU.mult)
                                     .wge(pe, c.at(f"bm_ptgmm_{i}_{g}")))
                                    c.mark(f"bm_ptg_{i}_{g}", dv)
                                    c.mark(f"bm_ptgused_{i}_{g}", dv)
                                    for hh in range(4):
                                        swa_head(c, i, g, hh, zb)
                                    u = i
                                    if g == 0:
                                        w = c.at(f"bm_ost_{i - 2}")
                                        if w:
                                            c.scalar.wait_ge(
                                                stSr[i % 2].h, w)
                                    (c.scalar.copy(
                                        ost[u % 2][:, g * 512:(g + 1) * 512],
                                        psO[g % 2][:, 0:512])
                                     .wge(pe,
                                          c.at(f"bm_omm_{i}_{g * 4 + 3}")))
                                    c.mark(f"bm_oevg_{i}_{g}", ac)
                                c.mark(f"bm_oev_{i}", ac)
                                khat_transpose(c, "bm", i, gTc, psTb)
                                state_update(c, "bm", i, psO[0], psO[1],
                                             extra_ac_wait=c.at(
                                                 f"bm_oevg_{i}_{KVH - 1}"))
                                c.mark(f"bm_kfree2_{i}", dv)
                                osem = stSr[i % 2]
                                w = c.at(f"bm_ost_{i - 2}")
                                if w:
                                    c.sync.wait_ge(osem.h, w)
                                (c.sync.dma_start(
                                    out=o_sp[:, :, (i - 1) * CHT:i * CHT]
                                    .rearrange("h d t -> d h t"),
                                    in_=ost[i % 2][:, :].rearrange(
                                        "p (h t) -> p h t", t=CHT))
                                 .wge(ac, c.at(f"bm_oev_{i}"))
                                 .inc(osem, 16))
                                c.mark(f"bm_ost_{i}", osem)
                        run_phase(ctx, block, bmain)

        if "B2" in phases:
            with contextlib.ExitStack() as sc_:
                Eb = sc_.enter_context
                ofm = Eb(nc.sbuf_tensor("b2_ofm", [128, 32 * 512], BF16))
                wosl = [Eb(nc.sbuf_tensor(f"b2_wo{i}", [128, 2048], BF16))
                        for i in range(2)]
                xch2 = Eb(nc.sbuf_tensor("b2_x", [128, DM], F32))
                att2 = Eb(nc.sbuf_tensor("b2_att", [128, 2 * DM], F32))
                bn6b = Eb(nc.sbuf_tensor("b2_bn6", [128, 48], F32))
                mvb = Eb(nc.sbuf_tensor("b2_mv", [128, 2], F32))
                rstdb = Eb(nc.sbuf_tensor("b2_rstd", [128, 1], F32))
                x1b = Eb(nc.sbuf_tensor("b2_x1", [128, DM], F32))
                psM = [Eb(nc.psum_tensor(f"b2_ps{i}", [128, 2048], F32))
                       for i in range(2)]
                ldQr2 = [sems["ldQ"], sems["ldQ1"]]

                def b2(c):
                    ebar(c, sems)
                    wu = [0]
                    xu = [0]
                    su = [0]
                    for b in range(4):
                        osem = ldSr[b % 2]
                        (c.sync.dma_start(
                            out=ofm[:, :].rearrange("p (h t) -> p h t",
                                                    t=512),
                            in_=o_sp[:, :, b * 512:(b + 1) * 512]
                            .rearrange("h d t -> d h t"))
                         .wge(pe, c.at(f"b2_mmdone_{b - 2}"))
                         .inc(osem, 16))
                        c.mark(f"b2_old_{b}", osem)
                        for tcg in range(2):
                            for fh in range(2):
                                w = c.at(f"b2_evp_{b}_{tcg}_{fh - 1}") or                                     c.at(f"b2_evp_{b}_{tcg - 1}_1") or                                     c.at(f"b2_evp_{b - 1}_1_1")
                                if w:
                                    c.tensor.wait_ge(ac.h, w)
                                for hdc in range(32):
                                    u = wu[0]
                                    slab = wosl[u % 2]
                                    wsem = ldGr[u % 2]
                                    (c.gpsimd.dma_start(
                                        out=slab[:, :],
                                        in_=wo_bfd[hdc * 128:(hdc + 1) * 128,
                                                   fh * 2048:(fh + 1) * 2048])
                                     .wge(pe, c.at(f"b2_wuse_{u - 2}"))
                                     .inc(wsem, 16))
                                    c.mark(f"b2_wld_{u}", wsem)
                                    if hdc == 0:
                                        c.tensor.wait_ge(osem.h,
                                                         c.at(f"b2_old_{b}"))
                                    insm = None
                                    for tc in range(2):
                                        tcx = tcg * 2 + tc
                                        for fg in range(4):
                                            insm = c.tensor.matmul(
                                                psM[tc][:, fg * 512:
                                                        (fg + 1) * 512],
                                                ofm[:, hdc * 512 + tcx * 128:
                                                    hdc * 512 +
                                                    (tcx + 1) * 128],
                                                slab[:, fg * 512:
                                                     (fg + 1) * 512],
                                                start=(hdc == 0),
                                                stop=(hdc == 31))
                                            if fg == 0 and tc == 0:
                                                insm.wge(wsem,
                                                         c.at(f"b2_wld_{u}"))
                                    insm.inc(pe)
                                    c.mark(f"b2_wuse_{u}", pe)
                                    wu[0] += 1
                                c.mark(f"b2_mm_{b}_{tcg}_{fh}", pe)
                                w = c.at(f"b2_pairdone_{b}_{tcg - 1}") or \
                                    c.at(f"b2_pairdone_{b - 1}_1")
                                if w:
                                    c.scalar.wait_ge(dv.h, w)
                                for tc in range(2):
                                    (c.scalar.copy(
                                        att2[:, tc * DM + fh * 2048:
                                             tc * DM + (fh + 1) * 2048],
                                        psM[tc][:, :])
                                     .wge(pe, c.at(f"b2_mm_{b}_{tcg}_{fh}")))
                                c.mark(f"b2_evp_{b}_{tcg}_{fh}", ac)
                            for tc in range(2):
                                tcx = tcg * 2 + tc
                                chunk = 1 + b * 4 + tcx
                                xsem = ldQr2[xu[0] % 2]
                                (c.sync.dma_start(
                                    out=xch2[:, :],
                                    in_=xin[chunk * CHT:(chunk + 1) * CHT, :])
                                 .wge(dv, c.at(f"b2_x1d_{xu[0] - 1}"))
                                 .inc(xsem, 16))
                                c.mark(f"b2_xld_{xu[0]}", xsem)
                                asl = att2[:, tc * DM:(tc + 1) * DM]
                                for bb in range(8):
                                    ins = c.vector.bn_stats(
                                        bn6b[:, 6 * bb:6 * bb + 6],
                                        asl[:, 512 * bb:512 * (bb + 1)])
                                    if bb == 0:
                                        ins.wge(ac, c.at(
                                            f"b2_evp_{b}_{tcg}_1"))
                                c.vector.bn_aggr(mvb[:, :], bn6b[:, :])
                                c.mark(f"b2_bn_{b}_{tcx}", dv)
                                (c.scalar.activation(
                                    out=rstdb[:, :], in_=mvb[:, 1:2],
                                    func=AF.Ln, bias=cst[:, 0:1], scale=1.0)
                                 .wge(dv, c.at(f"b2_bn_{b}_{tcx}")))
                                c.scalar.activation(
                                    out=rstdb[:, :], in_=rstdb[:, :],
                                    func=AF.Exp, scale=-0.5)
                                c.mark(f"b2_rstd_{b}_{tcx}", ac)
                                w = c.at(f"b2_x1st_{su[0] - 1}")
                                if w:
                                    c.vector.wait_ge(
                                        stSr[(su[0] - 1) % 2].h, w)
                                (c.vector.tensor_scalar(
                                    x1b[:, :], asl, mvb[:, 0:1],
                                    rstdb[:, :], ALU.subtract, ALU.mult)
                                 .wge(ac, c.at(f"b2_rstd_{b}_{tcx}")))
                                wx = c.at(f"b2_xld_{xu[0]}")
                                c.vector.wait_ge(xsem.h, wx)
                                c.vector.tensor_tensor(
                                    x1b[:, :], x1b[:, :], xch2[:, :],
                                    ALU.add)
                                c.mark(f"b2_x1d_{xu[0]}", dv)
                                xu[0] += 1
                                ssem = stSr[su[0] % 2]
                                w = c.at(f"b2_x1st_{su[0] - 2}")
                                if w:
                                    c.sync.wait_ge(ssem.h, w)
                                (c.sync.dma_start(
                                    out=x1_sp[(chunk - 1) * CHT:
                                              chunk * CHT, :],
                                    in_=x1b[:, :])
                                 .wge(dv, c.at(f"b2_x1d_{xu[0] - 1}"))
                                 .inc(ssem, 16))
                                c.mark(f"b2_x1st_{su[0]}", ssem)
                                su[0] += 1
                            c.mark(f"b2_pairdone_{b}_{tcg}", dv)
                        c.mark(f"b2_mmdone_{b}", pe)
                run_phase(ctx, block, b2)

        if "deq" in phases and cfg.get("deq_late"):
            emit_deq()

        if "C" in phases:
            with contextlib.ExitStack() as sc_:
                Ec = sc_.enter_context
                hfm = Ec(nc.sbuf_tensor("c_h", [128, HCF * 512], BF16))
                x1c = Ec(nc.sbuf_tensor("c_x1", [128, DM], F32))
                bn6c = Ec(nc.sbuf_tensor("c_bn6", [128, 48], F32))
                mvc = Ec(nc.sbuf_tensor("c_mv", [128, 2], F32))
                rstdc = Ec(nc.sbuf_tensor("c_rstd", [128, 1], F32))
                outb = Ec(nc.sbuf_tensor("c_out", [128, DM], F32))
                ldQr3 = [sems["ldQ"], sems["ldQ1"]]

                def cwait(c):
                    ebar(c, sems)
                    c.sync.wait_ge(cc.h, cc.n)
                    c.gpsimd.wait_ge(cc.h, cc.n)
                    c.tensor.wait_ge(cc.h, cc.n)
                run_phase(ctx, block, cwait)

                dq12v = dq12_g.rearrange("(r w dc p) h -> r w p dc h",
                                         w=2, dc=32, p=128)
                dq3v = dq3_g.rearrange("(r hc p) d -> r p hc d",
                                       hc=HCF, p=128)

                for b in range(4):
                    with contextlib.ExitStack() as ps_:
                        Ep = ps_.enter_context
                        ln2tm = Ep(nc.sbuf_tensor(f"c1_lntm{b}", [128, DM],
                                                  BF16))
                        ln2fm = Ep(nc.sbuf_tensor(f"c1_lnfm{b}",
                                                  [128, 32 * 512], BF16))
                        w12 = [Ep(nc.sbuf_tensor(f"c1_w{b}_{i}",
                                                 [128, 32 * 128], BF16))
                               for i in range(4)]
                        silub = Ep(nc.sbuf_tensor(f"c1_si{b}", [128, 512],
                                                  BF16))
                        psTc = Ep(nc.psum_tensor(f"c1_psT{b}", [128, 512],
                                                 BF16))
                        psG = [Ep(nc.psum_tensor(f"c1_psG{b}_{i}",
                                                 [128, 512], F32))
                               for i in range(4)]

                        def c1(c, b=b, ln2tm=ln2tm, ln2fm=ln2fm, w12=w12,
                               silub=silub, psTc=psTc, psG=psG):
                            ebar(c, sems)
                            lfm2 = ln2fm[:, :].rearrange(
                                "p (dc t) -> p dc t", t=512)
                            gT = [0]
                            for tc in range(4):
                                chunk = b * 4 + tc
                                xsem = ldQr3[tc % 2]
                                (c.sync.dma_start(
                                    out=x1c[:, :],
                                    in_=x1_sp[chunk * CHT:(chunk + 1) * CHT,
                                              :])
                                 .wge(dv, c.at(f"c1_lntm_{b}_{tc - 1}"))
                                 .inc(xsem, 16))
                                c.mark(f"c1_xld_{b}_{tc}", xsem)
                                for bb in range(8):
                                    ins = c.vector.bn_stats(
                                        bn6c[:, 6 * bb:6 * bb + 6],
                                        x1c[:, 512 * bb:512 * (bb + 1)])
                                    if bb == 0:
                                        ins.wge(xsem,
                                                c.at(f"c1_xld_{b}_{tc}"))
                                c.vector.bn_aggr(mvc[:, :], bn6c[:, :])
                                c.mark(f"c1_bn_{b}_{tc}", dv)
                                (c.scalar.activation(
                                    out=rstdc[:, :], in_=mvc[:, 1:2],
                                    func=AF.Ln, bias=cst[:, 0:1], scale=1.0)
                                 .wge(dv, c.at(f"c1_bn_{b}_{tc}")))
                                c.scalar.activation(
                                    out=rstdc[:, :], in_=rstdc[:, :],
                                    func=AF.Exp, scale=-0.5)
                                c.mark(f"c1_rstd_{b}_{tc}", ac)
                                w = c.at(f"c1_tr_{b}_{tc - 1}")
                                if w:
                                    c.vector.wait_ge(pe.h, w)
                                (c.vector.tensor_scalar(
                                    ln2tm[:, :], x1c[:, :], mvc[:, 0:1],
                                    rstdc[:, :], ALU.subtract, ALU.mult)
                                 .wge(ac, c.at(f"c1_rstd_{b}_{tc}")))
                                c.mark(f"c1_lntm_{b}_{tc}", dv)
                                for g in range(8):
                                    n = gT[0]
                                    w = c.at(f"c1_evT_{b}_{n - 1}")
                                    if w:
                                        c.tensor.wait_ge(dv.h, w)
                                    ins = None
                                    for k in range(4):
                                        dcix = g * 4 + k
                                        ins = c.tensor.transpose(
                                            psTc[:, k * 128:(k + 1) * 128],
                                            ln2tm[:, dcix * 128:
                                                  (dcix + 1) * 128],
                                            ident_bf[:, :])
                                        if k == 0:
                                            ins.wge(dv,
                                                    c.at(f"c1_lntm_{b}_{tc}"))
                                        ins.inc(pe)
                                    c.mark(f"c1_trT_{b}_{n}", pe)
                                    (c.vector.tensor_copy(
                                        lfm2[:, g * 4:g * 4 + 4,
                                             tc * 128:(tc + 1) * 128],
                                        psTc[:, :].rearrange(
                                            "p (k t) -> p k t", t=128))
                                     .wge(pe, c.at(f"c1_trT_{b}_{n}")))
                                    c.mark(f"c1_evT_{b}_{n}", dv)
                                    gT[0] += 1
                                c.mark(f"c1_tr_{b}_{tc}", pe)
                            c.mark(f"c1_lnfm_{b}", dv)
                            for hc in range(HCF):
                                rank, sub = hc // 11, hc % 11
                                for w_i in range(2):
                                    slab = w12[(hc % 2) * 2 + w_i]
                                    wsem = ldSr[hc % 2] if w_i == 0 \
                                        else [sems["ldV"],
                                              sems["ldV1"]][hc % 2]
                                    (c.sync.dma_start(
                                        out=slab[:, :].rearrange(
                                            "p (dc h) -> p dc h", h=128),
                                        in_=dq12v[rank, w_i, :, :,
                                                  sub * 128:(sub + 1) * 128])
                                     .wge(pe, c.at(f"c1_gu_{b}_{hc - 2}"))
                                     .inc(wsem, 16))
                                    c.mark(f"c1_wld_{b}_{hc}_{w_i}", wsem)
                                w = c.at(f"c1_hferm_{b}_{hc - 2}")
                                if w:
                                    c.tensor.wait_ge(dv.h, w)
                                if hc == 0:
                                    c.tensor.wait_ge(dv.h,
                                                     c.at(f"c1_lnfm_{b}"))
                                insm = None
                                for w_i in range(2):
                                    slab = w12[(hc % 2) * 2 + w_i]
                                    wsem = ldSr[hc % 2] if w_i == 0 \
                                        else [sems["ldV"],
                                              sems["ldV1"]][hc % 2]
                                    bank = psG[(hc % 2) * 2 + w_i]
                                    for dc in range(32):
                                        insm = c.tensor.matmul(
                                            bank[:, :],
                                            slab[:, dc * 128:(dc + 1) * 128],
                                            lfm2[:, dc, :],
                                            start=(dc == 0), stop=(dc == 31))
                                        if dc == 0:
                                            insm.wge(
                                                wsem,
                                                c.at(f"c1_wld_{b}_{hc}_{w_i}"))
                                    insm.inc(pe)
                                c.mark(f"c1_gu_{b}_{hc}", pe)
                                w = c.at(f"c1_hferm_{b}_{hc - 1}")
                                if w:
                                    c.scalar.wait_ge(dv.h, w)
                                (c.scalar.activation(
                                    out=silub[:, :],
                                    in_=psG[(hc % 2) * 2][:, :],
                                    func=AF.Silu)
                                 .wge(pe, c.at(f"c1_gu_{b}_{hc}")))
                                c.mark(f"c1_silu_{b}_{hc}", ac)
                                (c.vector.tensor_tensor(
                                    hfm[:, hc * 512:(hc + 1) * 512],
                                    psG[(hc % 2) * 2 + 1][:, :],
                                    silub[:, :], ALU.mult)
                                 .wge(ac, c.at(f"c1_silu_{b}_{hc}")))
                                c.mark(f"c1_hferm_{b}_{hc}", dv)
                            c.mark(f"c1_done_{b}", dv)
                        run_phase(ctx, block, c1)

                    with contextlib.ExitStack() as ps_:
                        Ep = ps_.enter_context
                        w3sl = [Ep(nc.sbuf_tensor(f"c2_w3{b}_{i}",
                                                  [128, HF], BF16))
                                for i in range(2)]
                        yst = Ep(nc.sbuf_tensor(f"c2_yst{b}", [128, 512],
                                                BF16))
                        ytm = Ep(nc.sbuf_tensor(f"c2_ytm{b}", [128, 4 * DM],
                                                BF16))
                        psD = [Ep(nc.psum_tensor(f"c2_psD{b}_{i}",
                                                 [128, 512], F32))
                               for i in range(2)]
                        psTc2 = Ep(nc.psum_tensor(f"c2_psT{b}", [128, 512],
                                                  BF16))

                        def c2(c, b=b, w3sl=w3sl, yst=yst, ytm=ytm, psD=psD,
                               psTc2=psTc2):
                            ebar(c, sems)
                            for dchunk in range(32):
                                rank = dchunk // 4
                                dloc = (dchunk % 4) * 128
                                slab = w3sl[dchunk % 2]
                                wsem = ldSr[dchunk % 2]
                                (c.sync.dma_start(
                                    out=slab[:, :].rearrange(
                                        "p (hc d) -> p hc d", d=128),
                                    in_=dq3v[rank, :, :,
                                             dloc:dloc + 128])
                                 .wge(pe, c.at(f"c2_dmm_{b}_{dchunk - 2}"))
                                 .inc(wsem, 16))
                                c.mark(f"c2_wld_{b}_{dchunk}", wsem)
                                bank = psD[dchunk % 2]
                                w = c.at(f"c2_ev_{b}_{dchunk - 2}")
                                if w:
                                    c.tensor.wait_ge(ac.h, w)
                                insm = None
                                for hc in range(HCF):
                                    insm = c.tensor.matmul(
                                        bank[:, :],
                                        slab[:, hc * 128:(hc + 1) * 128],
                                        hfm[:, hc * 512:(hc + 1) * 512],
                                        start=(hc == 0), stop=(hc == HCF - 1))
                                    if hc == 0:
                                        insm.wge(wsem,
                                                 c.at(f"c2_wld_{b}_{dchunk}"))
                                insm.inc(pe)
                                c.mark(f"c2_dmm_{b}_{dchunk}", pe)
                                w = c.at(f"c2_ytr_{b}_{dchunk - 1}")
                                if w:
                                    c.scalar.wait_ge(pe.h, w)
                                (c.scalar.copy(yst[:, :], bank[:, :])
                                 .wge(pe, c.at(f"c2_dmm_{b}_{dchunk}")))
                                c.mark(f"c2_ev_{b}_{dchunk}", ac)
                                w = c.at(f"c2_ytev_{b}_{dchunk - 1}")
                                if w:
                                    c.tensor.wait_ge(dv.h, w)
                                ins = None
                                for k in range(4):
                                    ins = c.tensor.transpose(
                                        psTc2[:, k * 128:(k + 1) * 128],
                                        yst[:, k * 128:(k + 1) * 128],
                                        ident_bf[:, :])
                                    if k == 0:
                                        ins.wge(ac,
                                                c.at(f"c2_ev_{b}_{dchunk}"))
                                    ins.inc(pe)
                                c.mark(f"c2_ytr_{b}_{dchunk}", pe)
                                (c.vector.tensor_copy(
                                    ytm[:, :].rearrange(
                                        "p (tc d) -> p tc d", d=DM
                                    )[:, :, dchunk * 128:(dchunk + 1) * 128],
                                    psTc2[:, :].rearrange(
                                        "p (k t) -> p k t", t=128))
                                 .wge(pe, c.at(f"c2_ytr_{b}_{dchunk}")))
                                c.mark(f"c2_ytev_{b}_{dchunk}", dv)
                            for tc in range(4):
                                chunk = b * 4 + tc
                                xsem = ldQr3[tc % 2]
                                (c.sync.dma_start(
                                    out=x1c[:, :],
                                    in_=x1_sp[chunk * CHT:(chunk + 1) * CHT,
                                              :])
                                 .wge(dv, c.at(f"c2_out_{b}_{tc - 1}"))
                                 .inc(xsem, 16))
                                c.mark(f"c2_xld_{b}_{tc}", xsem)
                                ysl = ytm[:, tc * DM:(tc + 1) * DM]
                                for bb in range(8):
                                    ins = c.vector.bn_stats(
                                        bn6c[:, 6 * bb:6 * bb + 6],
                                        ysl[:, 512 * bb:512 * (bb + 1)])
                                    if bb == 0:
                                        ins.wge(dv, c.at(
                                            f"c2_ytev_{b}_31"))
                                c.vector.bn_aggr(mvc[:, :], bn6c[:, :])
                                c.mark(f"c2_bn_{b}_{tc}", dv)
                                (c.scalar.activation(
                                    out=rstdc[:, :], in_=mvc[:, 1:2],
                                    func=AF.Ln, bias=cst[:, 0:1], scale=1.0)
                                 .wge(dv, c.at(f"c2_bn_{b}_{tc}")))
                                c.scalar.activation(
                                    out=rstdc[:, :], in_=rstdc[:, :],
                                    func=AF.Exp, scale=-0.5)
                                c.mark(f"c2_rstd_{b}_{tc}", ac)
                                w = c.at(f"c2_ost_{b}_{tc - 1}") if tc > 0 \
                                    else c.at(f"c2_ost_{b - 1}_3")
                                if w:
                                    c.vector.wait_ge(stSr[(tc - 1) % 2].h, w)
                                (c.vector.tensor_scalar(
                                    outb[:, :], ysl, mvc[:, 0:1],
                                    rstdc[:, :], ALU.subtract, ALU.mult)
                                 .wge(ac, c.at(f"c2_rstd_{b}_{tc}")))
                                wx = c.at(f"c2_xld_{b}_{tc}")
                                c.vector.wait_ge(xsem.h, wx)
                                c.vector.tensor_tensor(
                                    outb[:, :], outb[:, :], x1c[:, :],
                                    ALU.add)
                                c.mark(f"c2_out_{b}_{tc}", dv)
                                (c.sync.dma_start(
                                    out=x_out[chunk * CHT:(chunk + 1) * CHT,
                                              :],
                                    in_=outb[:, :])
                                 .wge(dv, c.at(f"c2_out_{b}_{tc}"))
                                 .inc(stSr[tc % 2], 16))
                                c.mark(f"c2_ost_{b}_{tc}", stSr[tc % 2])
                        run_phase(ctx, block, c2)

        def fin(c):
            for s_ in sems.values():
                if s_.n:
                    c.sync.wait_ge(s_.h, s_.n)
        run_phase(ctx, block, fin)

    return nc



def _stage_inputs(inputs):
    x = np.asarray(inputs["L0_attn_hidden_states"], dtype=np.float32)
    wq = np.ascontiguousarray(np.asarray(inputs["L0_attn_Wq"], np.float32))
    wk = np.ascontiguousarray(np.asarray(inputs["L0_attn_Wk"], np.float32))
    wv = np.ascontiguousarray(np.asarray(inputs["L0_attn_Wv"], np.float32))
    wo = np.ascontiguousarray(np.asarray(inputs["L0_attn_Wo"], np.float32))
    w1 = np.asarray(inputs["L0_ffn_w1"], np.float32)
    w2 = np.asarray(inputs["L0_ffn_w2"], np.float32)
    w3 = np.asarray(inputs["L0_ffn_w3"], np.float32)
    xf = np.ascontiguousarray(x.reshape(B_SZ * SEQ, DM))

    ident = np.eye(128, dtype=np.float32)
    a = np.arange(CHT)[:, None]
    kcol = np.arange(2 * CHT)[None, :]
    rel = (a + CHT) - kcol
    base = ((rel >= 0) & (rel <= 64)).astype(np.float32)
    m_rest = np.ascontiguousarray(base)
    m_first = np.ascontiguousarray(base * (kcol >= CHT).astype(np.float32))
    tri = np.ascontiguousarray(
        (np.arange(CHT)[:, None] <= np.arange(CHT)[None, :]).astype(np.float32))

    half = np.arange(HD // 2, dtype=np.float64)
    inv = 1.0 / THETA ** (2.0 * half / HD)
    inv2 = np.concatenate([inv, inv])

    in_maps = []
    for cid in range(NCORES):
        seq = cid // 4
        sl = cid % 4
        r0 = seq * SEQ + sl * TPC
        if sl == 0:
            xs = np.concatenate([np.zeros((CHT, DM), np.float32),
                                 xf[r0:r0 + TPC]], axis=0)
        else:
            xs = np.ascontiguousarray(xf[r0 - CHT:r0 + TPC])
        pos = np.arange(sl * TPC - CHT, sl * TPC + TPC, dtype=np.float64)
        pos = np.maximum(pos, 0.0)
        fr = inv2[:, None] * pos[None, :]
        cos_t = np.ascontiguousarray(np.cos(fr).astype(np.float32))
        sin_t = np.ascontiguousarray(np.sin(fr).astype(np.float32))
        r1 = cid * W1_SHARD
        w1c = np.zeros((W1_SHARD, DM), np.float32)
        w2c = np.zeros((W1_SHARD, DM), np.float32)
        n = max(0, min(W1_SHARD, HF - r1))
        if n > 0:
            w1c[:n] = w1[r1:r1 + n]
            w2c[:n] = w2[r1:r1 + n]
        w3c = np.ascontiguousarray(w3[cid * W3_SHARD:(cid + 1) * W3_SHARD])
        em = np.zeros((8, 8), np.float32)
        pm = np.zeros((8,), np.float32)
        for j in range(8):
            if j // 4 == seq and j < cid:
                pm[j] = 1.0
                for i in range(8):
                    if i // 4 == seq and j < i < cid:
                        em[i, j] = 1.0
        pmt = np.ascontiguousarray(np.tile(pm[None, :], (128, 1)).astype(np.float32))
        cv = np.zeros((128, 8), np.float32)
        cv[:, 0] = EPS
        cv[:, 1] = 1.0
        in_maps.append(dict(
            cvec=cv,
            xin=np.ascontiguousarray(xs), wq=wq, wk=wk, wv=wv, wo=wo,
            w1s=w1c, w2s=w2c, w3s=w3c,
            cos_t=cos_t, sin_t=sin_t, identf=ident,
            m_first=(m_first if sl == 0 else m_rest), m_rest=m_rest,
            tri_m=tri, emask=em, pmask_t=pmt,
        ))
    return in_maps


DEFAULT_CFG = {"phases": ("deq", "A", "Bpre", "B", "B2", "C"), "dbg": set(),
               "deq_late": True}


def kernel(**inputs):
    cfg = DEFAULT_CFG
    nc = build_program(cfg)
    in_maps = _stage_inputs(inputs)
    res = run_bass_kernel_spmd(nc, in_maps, core_ids=list(range(NCORES)))
    out = np.empty((B_SZ * SEQ, DM), np.float32)
    for cid in range(NCORES):
        seq, sl = cid // 4, cid % 4
        r0 = seq * SEQ + sl * TPC
        out[r0:r0 + TPC] = res.results[cid]["x_out"]
    return out.reshape(B_SZ, SEQ, DM)


# version 04 (final)
# speedup vs baseline: 1.0296x; 1.0296x over previous
import contextlib
import math

import numpy as np
import ml_dtypes

import concourse.bass as bass
import concourse.mybir as mybir
from concourse.bass_utils import run_bass_kernel_spmd

F32 = mybir.dt.float32
BF16 = mybir.dt.bfloat16
AF = mybir.ActivationFunctionType
ALU = mybir.AluOpType
AX = mybir.AxisListType

NCORES = 8
B_SZ = 2
SEQ = 8192
DM = 4096
TPC = 2048
CHT = 128
NCH = TPC // CHT
NCHH = NCH + 1
TPCH = NCHH * CHT
NHEAD = 32
KVH = 8
HD = 128
GRP = NHEAD // KVH
DKV = KVH * HD
HF = 11008
HCF = HF // 128
W1_SHARD = 1408
W3_SHARD = 512
THETA = 10000.0
EPS = 1e-5
INV_SQRT_HD = 1.0 / math.sqrt(HD)
INV_GN = 1.0 / 16.0

NF4_CODE = np.array([
    -1.0, -0.6961928009986877, -0.5250730514526367, -0.39491748809814453,
    -0.28444138169288635, -0.18477343022823334, -0.09105003625154495, 0.0,
    0.07958029955625534, 0.16093020141124725, 0.24611230194568634,
    0.33791524171829224, 0.44070982933044434, 0.5626170039176941,
    0.7229568362236023, 1.0], dtype=np.float64)
NF4_BOUNDS = [float(x) for x in (NF4_CODE[:-1] + NF4_CODE[1:]) * 0.5]
NF4_DELTA = [float(x) for x in np.diff(NF4_CODE)]

ENGINES = ("tensor", "vector", "scalar", "gpsimd", "sync")


class SemT:
    def __init__(self, h, name):
        self.h = h
        self.name = name
        self.n = 0


class Ins:
    __slots__ = ("i",)

    def __init__(self, i):
        self.i = i

    def inc(self, sem, k=1):
        sem.n += k
        if self.i is not None:
            self.i.then_inc(sem.h, k)
        return self

    def wge(self, sem, val):
        if self.i is not None and val > 0:
            self.i._wait_ge(sem.h, val)
        return self


NOAUTO = {"wait_ge", "sem_inc", "sem_clear", "nop", "dma_start",
          "dma_start_transpose", "collective_compute"}


class Eng:

    def __init__(self, real, name=None, selfsem=None):
        self._e = real
        self._name = name
        self._ss = selfsem

    def __getattr__(self, m):
        e = self._e
        ss = self._ss

        def call(*a, **kw):
            if ss is not None and m not in NOAUTO:
                if e is not None and ss.n > 0:
                    e.wait_ge(ss.h, ss.n)
                ins = getattr(e, m)(*a, **kw) if e is not None else None
                ss.n += 1
                if ins is not None:
                    ins.then_inc(ss.h, 1)
                return Ins(ins)
            if e is None:
                return Ins(None)
            return Ins(getattr(e, m)(*a, **kw))
        return call


class Ctx:
    def __init__(self, sems):
        self.sems = sems
        self.marks = {}
        self.counting = False
        self.engs = {}

    def __getattr__(self, name):
        if name in ENGINES:
            return self.engs[name]
        raise AttributeError(name)

    def mark(self, name, sem):
        if self.counting:
            self.marks[name] = sem.n
        return sem.n

    def at(self, name):
        return self.marks.get(name, 0)


SELFSEM = {"vector": "dv", "scalar": "ac"}


def _mk_engs(ctx, real_map):
    engs = {}
    for e in ENGINES:
        ss = ctx.sems[SELFSEM[e]] if e in SELFSEM else None
        engs[e] = Eng(real_map.get(e), e, ss)
    return engs


def run_phase(ctx, block, emitter):
    start = {k: s.n for k, s in ctx.sems.items()}
    ctx.counting = True
    ctx.engs = _mk_engs(ctx, {})
    emitter(ctx)
    end = {k: s.n for k, s in ctx.sems.items()}
    ctx.counting = False
    for ename in ENGINES:
        for k, s in ctx.sems.items():
            s.n = start[k]

        def body(real_eng, ename=ename):
            ctx.engs = _mk_engs(ctx, {ename: real_eng})
            emitter(ctx)
        getattr(block, ename)(body)
        for k, s in ctx.sems.items():
            assert s.n == end[k], (ename, k, s.n, end[k])


def ebar(c, sems):
    for e in ENGINES:
        eng = getattr(c, e)
        for s_ in sems.values():
            if s_.n > 0:
                eng.wait_ge(s_.h, s_.n)



def build_program(cfg):
    nc = bass.Bass("TRN2", target_bir_lowering=False, debug=False)
    dbg = cfg.get("dbg", set())
    phases = cfg.get("phases", ("deq", "A", "Bpre", "B", "B2", "C"))

    def din(name, shape, dtype=F32):
        return nc.dram_tensor(name, shape, dtype, kind="ExternalInput").ap()

    def dscr(name, shape, dtype, dbgkey=None, shared=False):
        if dbgkey and dbgkey in dbg:
            return nc.dram_tensor(name, shape, dtype, kind="ExternalOutput").ap()
        if shared:
            return nc.dram_tensor(name, shape, dtype, addr_space="Shared").ap()
        return nc.dram_tensor(name, shape, dtype).ap()

    xin = din("xin", [TPCH, DM])
    wq = din("wq", [DM, DM])
    wk = din("wk", [DM, DKV])
    wv = din("wv", [DM, DKV])
    wo = din("wo", [DM, DM])
    w1s = din("w1s", [W1_SHARD, DM])
    w2s = din("w2s", [W1_SHARD, DM])
    w3s = din("w3s", [W3_SHARD, HF])
    cos_t = din("cos_t", [HD, TPCH])
    sin_t = din("sin_t", [HD, TPCH])
    identf = din("identf", [128, 128])
    m_first = din("m_first", [CHT, 2 * CHT])
    m_rest = din("m_rest", [CHT, 2 * CHT])
    tri_m = din("tri_m", [CHT, CHT])
    emask = din("emask", [8, 8])
    pmask_t = din("pmask_t", [128, 8])
    cvec = din("cvec", [128, 8])

    x_out = nc.dram_tensor("x_out", [TPC, DM], F32, kind="ExternalOutput").ap()

    q_sp = dscr("q_sp", [NHEAD, HD, TPCH], BF16, "dbgspill")
    k_sp = dscr("k_sp", [KVH, HD, TPCH], BF16, "dbgspill")
    v_sp = dscr("v_sp", [TPCH, DKV], BF16, "dbgspill")
    o_sp = dscr("o_sp", [NHEAD, HD, TPC], BF16, "dbgo")
    x1_sp = dscr("x1_sp", [TPC, DM], F32, "dbgx1")
    wo_bfd = dscr("wo_bfd", [DM, DM], BF16)
    dq12_loc = dscr("dq12_loc", [2 * DM, W1_SHARD], BF16, "dbgdq")
    dq3_loc = dscr("dq3_loc", [HF, W3_SHARD], BF16, "dbgdq")
    dq12_g = dscr("dq12_g", [NCORES * 2 * DM, W1_SHARD], BF16, "dbgdqg",
                  shared=True)
    dq3_g = dscr("dq3_g", [NCORES * HF, W3_SHARD], BF16, shared=True)
    st_loc = dscr("st_loc", [KVH * HD + KVH, HD], F32, "dbgst")
    st_g = dscr("st_g", [NCORES * (KVH * HD + KVH), HD], F32, "dbgst",
                shared=True)

    with contextlib.ExitStack() as top:
        E = top.enter_context
        sems = {}
        for nm in ("ldS", "ldG", "pe", "dv", "ac", "cc", "stS",
                   "ldS1", "ldG1", "stS1", "ldP", "ldQ", "ldQ1", "ldV", "ldV1", "ldM"):
            sems[nm] = SemT(E(nc.semaphore("sem_" + nm)), nm)
        ctx = Ctx(sems)
        ldS, ldG = sems["ldS"], sems["ldG"]
        pe, dv, ac = sems["pe"], sems["dv"], sems["ac"]
        cc, stS = sems["cc"], sems["stS"]
        ldSr = [sems["ldS"], sems["ldS1"]]
        ldGr = [sems["ldG"], sems["ldG1"]]
        stSr = [sems["stS"], sems["stS1"]]
        ldP = sems["ldP"]

        block = E(nc.Block())

        ident_bf = E(nc.sbuf_tensor("ident_bf", [128, 128], BF16))
        ident_f = E(nc.sbuf_tensor("ident_f", [128, 128], F32))
        ones_col = E(nc.sbuf_tensor("ones_col", [128, 1], BF16))
        ones_row = E(nc.sbuf_tensor("ones_row", [1, 128], F32))
        cst = E(nc.sbuf_tensor("cst", [128, 8], F32))
        S_f = E(nc.sbuf_tensor("S_f", [128, KVH * 128], F32))
        S_bf = E(nc.sbuf_tensor("S_bf", [128, KVH * 128], BF16))
        aT_acc = E(nc.sbuf_tensor("aT_acc", [128, KVH], F32))

        pre_stack = contextlib.ExitStack()
        wopre = [pre_stack.enter_context(
            nc.sbuf_tensor(f"wopre{i}", [128, DM], BF16)) for i in range(4)]

        def pre(c):
            c.gpsimd.dma_start(out=ident_bf[:, :], in_=identf[:, :]).inc(ldG, 16)
            c.sync.dma_start(out=cst[:, :], in_=cvec[:, :]).inc(ldP, 16)
            (c.sync.dma_start(out=ident_f[:, :], in_=identf[:, :])
             .wge(ldP, ldP.n).inc(ldP, 16))
            for i in range(32):
                wsem = ldGr[i % 2]
                ssem = stSr[i % 2]
                (c.gpsimd.dma_start(
                    out=wopre[i % 4][:, :],
                    in_=wo[i * 128:(i + 1) * 128, :])
                 .wge(ssem, c.at(f"pre_wost_{i - 4}"))
                 .inc(wsem, 16))
                c.mark(f"pre_wold_{i}", wsem)
                (c.sync.dma_start(
                    out=wo_bfd[i * 128:(i + 1) * 128, :],
                    in_=wopre[i % 4][:, :])
                 .wge(wsem, c.at(f"pre_wold_{i}"))
                 .inc(ssem, 16))
                c.mark(f"pre_wost_{i}", ssem)
            c.vector.memset(ones_col[:, :], 1.0)
            c.vector.memset(ones_row[:, :], 1.0)
            c.vector.memset(S_f[:, :], 0.0)
            c.vector.memset(S_bf[:, :], 0.0)
            c.vector.memset(aT_acc[:, :], 0.0)
        run_phase(ctx, block, pre)
        pre_stack.close()

        def emit_deq():
            with contextlib.ExitStack() as sc_:
                Ed = sc_.enter_context
                wt = Ed(nc.sbuf_tensor("dq_wt", [128, HF], F32))
                am = Ed(nc.sbuf_tensor("dq_am", [128, HF // 64], F32))
                rc = Ed(nc.sbuf_tensor("dq_rc", [128, HF // 64], F32))
                nrm = Ed(nc.sbuf_tensor("dq_nrm", [128, HF], F32))
                acc0 = Ed(nc.sbuf_tensor("dq_acc0", [128, HF], BF16))
                acc1 = Ed(nc.sbuf_tensor("dq_acc1", [128, HF], BF16))
                dqv = [Ed(nc.sbuf_tensor(f"dq_dqv{i}", [128, HF], BF16))
                       for i in range(2)]
                tstg = [Ed(nc.sbuf_tensor(f"dq_tstg{i}", [128, 512], BF16))
                        for i in range(2)]
                pst = [Ed(nc.psum_tensor(f"dq_pst{i}", [128, 512], BF16))
                       for i in range(2)]

                jobs = []
                for r in range(W1_SHARD // 128):
                    jobs.append((w1s[r * 128:(r + 1) * 128, :], DM, 0, r))
                for r in range(W1_SHARD // 128):
                    jobs.append((w2s[r * 128:(r + 1) * 128, :], DM, 1, r))
                for r in range(W3_SHARD // 128):
                    jobs.append((w3s[r * 128:(r + 1) * 128, :], HF, 2, r))

                def deq(c):
                    ebar(c, sems)
                    gcnt = 0
                    for i, (src, F, which, r) in enumerate(jobs):
                        nb = F // 64
                        ncks = F // 128
                        (c.sync.dma_start(out=wt[:, :F], in_=src)
                         .wge(dv, c.at(f"dq_nrm_{i - 1}"))
                         .inc(ldS, 16))
                        c.mark(f"dq_ld_{i}", ldS)
                        (c.vector.reduce_max(
                            am[:, :nb],
                            wt[:, :F].rearrange("p (b w) -> p b w", w=64),
                            axis=AX.X, apply_absolute_value=True)
                         .wge(ldS, c.at(f"dq_ld_{i}")))
                        c.vector.tensor_scalar(rc[:, :nb], am[:, :nb],
                                               1e-30, None, ALU.max)
                        c.vector.reciprocal(rc[:, :nb], rc[:, :nb])
                        c.vector.tensor_tensor(
                            nrm[:, :F].rearrange("p (b w) -> p b w", w=64),
                            wt[:, :F].rearrange("p (b w) -> p b w", w=64),
                            rc[:, :nb].rearrange("p (b o) -> p b o", o=1
                                                 ).broadcast_to([128, nb, 64]),
                            ALU.mult)
                        c.mark(f"dq_nrm_{i}", dv)
                        tmp = dqv[i % 2]
                        c.vector.tensor_scalar(acc0[:, :F], nrm[:, :F],
                                               NF4_BOUNDS[0], NF4_DELTA[0],
                                               ALU.is_gt, ALU.mult)
                        c.vector.tensor_scalar(acc1[:, :F], nrm[:, :F],
                                               NF4_BOUNDS[1], NF4_DELTA[1],
                                               ALU.is_gt, ALU.mult)
                        for j in range(2, 15):
                            tgt = acc0 if (j % 2 == 0) else acc1
                            (c.vector.tensor_scalar(tmp[:, :F], nrm[:, :F],
                                                    NF4_BOUNDS[j], NF4_DELTA[j],
                                                    ALU.is_gt, ALU.mult)
                             .wge(pe, c.at(f"dq_tr_{i - 2}") if j == 2 else 0))
                            c.vector.tensor_tensor(tgt[:, :F], tgt[:, :F],
                                                   tmp[:, :F], ALU.add)
                        c.vector.tensor_tensor(acc0[:, :F], acc0[:, :F],
                                               acc1[:, :F], ALU.add)
                        c.vector.scalar_tensor_tensor(
                            tmp[:, :F].rearrange("p (b w) -> p b w", w=64),
                            acc0[:, :F].rearrange("p (b w) -> p b w", w=64),
                            -1.0,
                            am[:, :nb].rearrange("p (b o) -> p b o", o=1
                                                 ).broadcast_to([128, nb, 64]),
                            ALU.add, ALU.mult)
                        c.mark(f"dq_dq_{i}", dv)
                        g4 = (ncks + 3) // 4
                        for g in range(g4):
                            kk = min(4, ncks - g * 4)
                            bank = pst[gcnt % 2]
                            ssem = stSr[gcnt % 2]
                            w = c.at(f"dq_ev_{gcnt - 2}")
                            if w:
                                c.tensor.wait_ge(ac.h, w)
                            ins = None
                            for k in range(kk):
                                cix = g * 4 + k
                                ins = c.tensor.transpose(
                                    bank[:, k * 128:(k + 1) * 128],
                                    tmp[:, cix * 128:(cix + 1) * 128],
                                    ident_bf[:, :])
                                if k == 0:
                                    ins.wge(dv, c.at(f"dq_dq_{i}"))
                                ins.inc(pe)
                            c.mark(f"dq_trp_{gcnt}", pe)
                            if which < 2:
                                o_ap = dq12_loc.rearrange(
                                    "(w dc p) h -> p w dc h", w=2, p=128
                                )[:, which, g * 4:g * 4 + kk,
                                  r * 128:(r + 1) * 128]
                            else:
                                o_ap = dq3_loc.rearrange(
                                    "(dc p) h -> p dc h", p=128
                                )[:, g * 4:g * 4 + kk, r * 128:(r + 1) * 128]
                            w = c.at(f"dq_st_{gcnt - 2}")
                            if w:
                                c.scalar.wait_ge(ssem.h, w)
                            (c.scalar.copy(tstg[gcnt % 2][:, :kk * 128],
                                           bank[:, :kk * 128])
                             .wge(pe, c.at(f"dq_trp_{gcnt}")))
                            c.mark(f"dq_ev_{gcnt}", ac)
                            (c.sync.dma_start(
                                out=o_ap,
                                in_=tstg[gcnt % 2][:, :kk * 128].rearrange(
                                    "p (dc h) -> p dc h", dc=kk))
                             .wge(ac, c.at(f"dq_ev_{gcnt}"))
                             .inc(ssem, 16))
                            c.mark(f"dq_st_{gcnt}", ssem)
                            gcnt += 1
                        c.mark(f"dq_tr_{i}", pe)
                    c.gpsimd.wait_ge(stSr[0].h, stSr[0].n)
                    c.gpsimd.wait_ge(stSr[1].h, stSr[1].n)
                    (c.gpsimd.collective_compute(
                        "AllGather", ALU.bypass,
                        replica_groups=[list(range(NCORES))],
                        ins=[dq12_loc[:, :].opt()],
                        outs=[dq12_g[:, :].opt()])
                     .inc(cc))
                    (c.gpsimd.collective_compute(
                        "AllGather", ALU.bypass,
                        replica_groups=[list(range(NCORES))],
                        ins=[dq3_loc[:, :].opt()],
                        outs=[dq3_g[:, :].opt()])
                     .inc(cc))
                    c.mark("dq_cc", cc)
                run_phase(ctx, block, deq)

        if "deq" in phases and not cfg.get("deq_late"):
            emit_deq()

        if "A" in phases:
            halves = [(0, 9), (9, 8)]
            nhmax = 9
            with contextlib.ExitStack() as sc_:
                Ea = sc_.enter_context
                xch = [Ea(nc.sbuf_tensor(f"a_x{i}", [128, DM], F32))
                       for i in range(2)]
                ln1tm = Ea(nc.sbuf_tensor("a_ln1tm", [128, DM], BF16))
                ln1fm = Ea(nc.sbuf_tensor("a_ln1fm", [128, 32 * nhmax * CHT],
                                          BF16))
                bn6 = Ea(nc.sbuf_tensor("a_bn6", [128, 48], F32))
                mv = Ea(nc.sbuf_tensor("a_mv", [128, 2], F32))
                rstd = Ea(nc.sbuf_tensor("a_rstd", [128, 1], F32))
                wsl = [Ea(nc.sbuf_tensor(f"a_wsl{i}", [128, 32 * 256], BF16))
                       for i in range(2)]
                wvsl = [Ea(nc.sbuf_tensor(f"a_wvsl{i}", [128, DKV], BF16))
                        for i in range(2)]
                stg = [Ea(nc.sbuf_tensor(f"a_stg{i}", [128, 512], BF16))
                       for i in range(2)]

                def phase_a1(c, hi, ch0, nch, psT, psQ, gT, tg_ct):
                    Th = nch * CHT
                    lfm = ln1fm[:, :].rearrange("p (dc t) -> p dc t", t=Th)
                    if True:
                        for j in range(nch):
                            chunk = ch0 + j
                            xb = xch[j % 2]
                            xsem = ldSr[j % 2]
                            (c.sync.dma_start(
                                out=xb[:, :],
                                in_=xin[chunk * CHT:(chunk + 1) * CHT, :])
                             .wge(dv, c.at(f"a_lntm_{hi}_{j - 2}"))
                             .inc(xsem, 16))
                            c.mark(f"a_xld_{hi}_{j}", xsem)
                            for b in range(8):
                                ins = c.vector.bn_stats(
                                    bn6[:, 6 * b:6 * b + 6],
                                    xb[:, 512 * b:512 * (b + 1)])
                                if b == 0:
                                    ins.wge(xsem, c.at(f"a_xld_{hi}_{j}"))
                            c.vector.bn_aggr(mv[:, :], bn6[:, :])
                            c.mark(f"a_bn_{hi}_{j}", dv)
                            (c.scalar.activation(out=rstd[:, :], in_=mv[:, 1:2],
                                                 func=AF.Ln, bias=cst[:, 0:1],
                                                 scale=1.0)
                             .wge(dv, c.at(f"a_bn_{hi}_{j}")))
                            c.scalar.activation(out=rstd[:, :], in_=rstd[:, :],
                                                func=AF.Exp, scale=-0.5)
                            c.mark(f"a_rstd_{hi}_{j}", ac)
                            if j > 0:
                                w = c.at(f"a_trj_{hi}_{j - 1}")
                                if w:
                                    c.vector.wait_ge(pe.h, w)
                            (c.vector.tensor_scalar(
                                ln1tm[:, :], xb[:, :], mv[:, 0:1], rstd[:, :],
                                ALU.subtract, ALU.mult)
                                .wge(ac, c.at(f"a_rstd_{hi}_{j}"))
                                )
                            c.mark(f"a_lntm_{hi}_{j}", dv)
                            for g in range(8):
                                bank = psT[gT[0] % 2]
                                w = c.at(f"a_evT_{hi}_{gT[0] - 2}")
                                if w:
                                    c.tensor.wait_ge(ac.h, w)
                                ins = None
                                for k in range(4):
                                    dcix = g * 4 + k
                                    ins = c.tensor.transpose(
                                        bank[:, k * 128:(k + 1) * 128],
                                        ln1tm[:, dcix * 128:(dcix + 1) * 128],
                                        ident_bf[:, :])
                                    if k == 0:
                                        ins.wge(dv, c.at(f"a_lntm_{hi}_{j}"))
                                    ins.inc(pe)
                                c.mark(f"a_trT_{hi}_{gT[0]}", pe)
                                (c.scalar.copy(
                                    lfm[:, g * 4:g * 4 + 4,
                                        j * CHT:(j + 1) * CHT],
                                    bank[:, :].rearrange("p (k t) -> p k t",
                                                         t=128))
                                 .wge(pe, c.at(f"a_trT_{hi}_{gT[0]}")))
                                c.mark(f"a_evT_{hi}_{gT[0]}", ac)
                                gT[0] += 1
                            c.mark(f"a_trj_{hi}_{j}", pe)
                        c.mark(f"a_lnfmdone_{hi}", ac)
                        if nch == 9:
                            tgs = [(0, 512), (512, 512), (1024, 128)]
                        else:
                            tgs = [(0, 512), (512, 512)]
                        pairs = [(wq, q_sp, f) for f in range(16)] + \
                                [(wk, k_sp, f) for f in range(4)]
                        for pi, (wsrc, dst, f2) in enumerate(pairs):
                            slab = wsl[pi % 2]
                            wsem = ldGr[pi % 2]
                            (c.gpsimd.dma_start(
                                out=slab[:, :].rearrange("p (a f) -> p a f",
                                                         f=256),
                                in_=wsrc[:, f2 * 256:(f2 + 1) * 256].rearrange(
                                    "(a p) f -> p a f", p=128))
                             .wge(pe, c.at(f"a_qkpair_{hi}_{pi - 2}"))
                             .inc(wsem, 16))
                            c.mark(f"a_wld_{hi}_{pi}", wsem)
                            slabv = slab[:, :].rearrange("p (a f) -> p a f",
                                                         f=256)
                            for fc in range(2):
                                fci = f2 * 2 + fc
                                for ti, (t0, tw) in enumerate(tgs):
                                    u = tg_ct[0]
                                    bank = psQ[u % 2]
                                    w = max(c.at(f"a_qkev_{hi}_{u - 2}"),
                                            c.at(f"a_lnfmdone_{hi}")
                                            if (pi == 0 and fc == 0
                                                and ti == 0) else 0)
                                    if w:
                                        c.tensor.wait_ge(ac.h, w)
                                    ins = None
                                    for dc in range(32):
                                        ins = c.tensor.matmul(
                                            bank[:, :tw],
                                            slabv[:, dc,
                                                  fc * 128:(fc + 1) * 128],
                                            lfm[:, dc, t0:t0 + tw],
                                            start=(dc == 0), stop=(dc == 31))
                                        if dc == 0:
                                            ins.wge(wsem,
                                                    c.at(f"a_wld_{hi}_{pi}"))
                                    ins.inc(pe)
                                    c.mark(f"a_qkmm_{hi}_{u}", pe)
                                    sg = stg[u % 2]
                                    w = c.at(f"a_qkst_{hi}_{u - 2}")
                                    if w:
                                        c.scalar.wait_ge(stSr[u % 2].h, w)
                                    (c.scalar.copy(sg[:, :tw], bank[:, :tw])
                                     .wge(pe, c.at(f"a_qkmm_{hi}_{u}")))
                                    c.mark(f"a_qkev_{hi}_{u}", ac)
                                    (c.sync.dma_start(
                                        out=dst[fci, :,
                                                ch0 * CHT + t0:
                                                ch0 * CHT + t0 + tw],
                                        in_=sg[:, :tw])
                                     .wge(ac, c.at(f"a_qkev_{hi}_{u}"))
                                     .inc(stSr[u % 2], 16))
                                    c.mark(f"a_qkst_{hi}_{u}", stSr[u % 2])
                                    tg_ct[0] += 1
                            c.mark(f"a_qkpair_{hi}_{pi}", pe)

                def phase_a2(c, hi, ch0, nch, psV):
                    Th = nch * CHT
                    lfm = ln1fm[:, :].rearrange("p (dc t) -> p dc t", t=Th)
                    if True:
                        vu = [0]
                        ngr = (nch + 3) // 4
                        for gi in range(ngr):
                            tcs = list(range(gi * 4, min(nch, gi * 4 + 4)))
                            for dc in range(32):
                                vdc = gi * 32 + dc
                                slab = wvsl[vdc % 2]
                                vsem = ldGr[vdc % 2]
                                (c.gpsimd.dma_start(
                                    out=slab[:, :],
                                    in_=wv[dc * 128:(dc + 1) * 128, :])
                                 .wge(pe,
                                      c.at(f"a_vmm_{hi}_{vdc - 2}"))
                                 .inc(vsem, 16))
                                c.mark(f"a_vw_{hi}_{vdc}", vsem)
                                if dc == 0:
                                    w = c.at(f"a_vevdone_{hi}_{gi - 1}")
                                    if w:
                                        c.tensor.wait_ge(ac.h, w)
                                ins = None
                                for k, tc in enumerate(tcs):
                                    for vf in range(2):
                                        ins = c.tensor.matmul(
                                            psV[k * 2 + vf][:, :],
                                            lfm[:, dc, tc * CHT:(tc + 1) * CHT],
                                            slab[:, vf * 512:(vf + 1) * 512],
                                            start=(dc == 0), stop=(dc == 31))
                                        if k == 0 and vf == 0:
                                            ins.wge(vsem, c.at(
                                                f"a_vw_{hi}_{vdc}"))
                                ins.inc(pe)
                                c.mark(f"a_vmm_{hi}_{vdc}", pe)
                            for k, tc in enumerate(tcs):
                                for vf in range(2):
                                    u = vu[0]
                                    sg = stg[u % 2]
                                    w = c.at(f"a_vst_{hi}_{u - 2}")
                                    if w:
                                        c.scalar.wait_ge(stSr[u % 2].h, w)
                                    (c.scalar.activation(
                                        out=sg[:, :512],
                                        in_=psV[k * 2 + vf][:, :],
                                        func=AF.Copy, scale=0.5)
                                     .wge(pe, c.at(f"a_vmm_{hi}_{gi * 32 + 31}")))
                                    c.mark(f"a_vev_{hi}_{u}", ac)
                                    (c.sync.dma_start(
                                        out=v_sp[(ch0 + tc) * CHT:
                                                 (ch0 + tc + 1) * CHT,
                                                 vf * 512:(vf + 1) * 512],
                                        in_=sg[:, :512])
                                     .wge(ac, c.at(f"a_vev_{hi}_{u}"))
                                     .inc(stSr[u % 2], 16))
                                    c.mark(f"a_vst_{hi}_{u}", stSr[u % 2])
                                    vu[0] += 1
                            c.mark(f"a_vevdone_{hi}_{gi}", ac)
                for hi, (ch0, nch) in enumerate(halves):
                    with contextlib.ExitStack() as ps_:
                        Ep = ps_.enter_context
                        psT = [Ep(nc.psum_tensor(f"a_psT{hi}_{i}", [128, 512],
                                                 BF16)) for i in range(2)]
                        psQ = [Ep(nc.psum_tensor(f"a_psQ{hi}_{i}", [128, 512],
                                                 F32)) for i in range(2)]

                        def a1(c, hi=hi, ch0=ch0, nch=nch, psT=psT, psQ=psQ):
                            ebar(c, sems)
                            phase_a1(c, hi, ch0, nch, psT, psQ, [0], [0])
                        run_phase(ctx, block, a1)
                    with contextlib.ExitStack() as ps_:
                        Ep = ps_.enter_context
                        psV = [Ep(nc.psum_tensor(f"a_psV{hi}_{i}", [128, 512],
                                                 F32)) for i in range(8)]

                        def a2(c, hi=hi, ch0=ch0, nch=nch, psV=psV):
                            ebar(c, sems)
                            phase_a2(c, hi, ch0, nch, psV)
                        run_phase(ctx, block, a2)

        if "Bpre" in phases or "B" in phases:
            with contextlib.ExitStack() as sc_:
                Eb = sc_.enter_context
                cosb = Eb(nc.sbuf_tensor("b_cos", [128, TPCH], F32))
                sinb = Eb(nc.sbuf_tensor("b_sin", [128, TPCH], F32))
                kc = [Eb(nc.sbuf_tensor(f"b_kc{i}", [128, DKV], BF16))
                      for i in range(2)]
                vc = [Eb(nc.sbuf_tensor(f"b_vc{i}", [128, DKV], BF16))
                      for i in range(2)]
                krr = [Eb(nc.sbuf_tensor(f"b_kr{i}", [128, DKV], BF16))
                       for i in range(2)]
                gsv = Eb(nc.sbuf_tensor("b_gs", [128, DKV], F32))
                Av = Eb(nc.sbuf_tensor("b_A", [128, DKV], F32))
                ekv = Eb(nc.sbuf_tensor("b_ek", [128, DKV], BF16))
                emA = Eb(nc.sbuf_tensor("b_emA", [128, DKV], BF16))
                ktl = Eb(nc.sbuf_tensor("b_kt", [128, DKV], BF16))
                khat = Eb(nc.sbuf_tensor("b_khat", [128, DKV], BF16))
                khat_t = Eb(nc.sbuf_tensor("b_khatt", [128, DKV], BF16))
                eCd = Eb(nc.sbuf_tensor("b_eC", [128, KVH], F32))
                zrow = Eb(nc.sbuf_tensor("b_zrow", [1, DM + DKV], F32))
                zbc = [Eb(nc.sbuf_tensor(f"b_zbc{i}", [128, 512], F32))
                       for i in range(2)]
                tmpk = Eb(nc.sbuf_tensor("b_tmpk", [128, DKV], F32))
                mf_sb = Eb(nc.sbuf_tensor("b_mf", [128, 256], BF16))
                mr_sb = Eb(nc.sbuf_tensor("b_mr", [128, 256], BF16))
                tri_sb = Eb(nc.sbuf_tensor("b_tri", [128, 128], BF16))

                ldQr = [sems["ldQ"], sems["ldQ1"]]

                def kv_load(c, pref, i, kwaitD, kwaitA, vwait):
                    ksem = ldSr[i % 2]
                    vsem = [sems["ldV"], sems["ldV1"]][i % 2]
                    tr = slice(i * CHT, (i + 1) * CHT)
                    if kwaitA:
                        c.sync.wait_ge(ac.h, kwaitA)
                    (c.sync.dma_start(
                        out=kc[i % 2][:, :].rearrange("p (g t) -> p g t",
                                                      t=CHT),
                        in_=k_sp[:, :, tr].rearrange("g h t -> h g t"))
                     .wge(dv, kwaitD).inc(ksem, 16))
                    c.mark(f"{pref}_kld_{i}", ksem)
                    (c.sync.dma_start(out=vc[i % 2][:, :], in_=v_sp[tr, :])
                     .wge(pe, vwait).inc(vsem, 16))
                    c.mark(f"{pref}_vld_{i}", vsem)

                def k_rope(c, pref, i):
                    kci = kc[i % 2][:, :]
                    tr = slice(i * CHT, (i + 1) * CHT)
                    (c.vector.tensor_scalar(tmpk[0:64, :], kci[64:128, :],
                                            -1.0, None, ALU.mult)
                     .wge(ldSr[i % 2], c.at(f"{pref}_kld_{i}")))
                    c.vector.tensor_scalar(tmpk[64:128, :], kci[0:64, :],
                                           1.0, None, ALU.mult)
                    k3 = kci.rearrange("p (g t) -> p g t", t=CHT)
                    t3 = tmpk[:, :].rearrange("p (g t) -> p g t", t=CHT)
                    cs3 = cosb[:, tr].rearrange("p (o t) -> p o t", o=1
                                                ).broadcast_to([128, KVH, CHT])
                    sn3 = sinb[:, tr].rearrange("p (o t) -> p o t", o=1
                                                ).broadcast_to([128, KVH, CHT])
                    kr3 = krr[i % 2][:, :].rearrange("p (g t) -> p g t", t=CHT)
                    c.vector.tensor_tensor(t3, t3, sn3, ALU.mult)
                    c.vector.tensor_tensor(kr3, k3, cs3, ALU.mult)
                    c.vector.tensor_tensor(kr3, kr3, t3, ALU.add)
                    c.mark(f"{pref}_krope_{i}", dv)

                def k_gate(c, pref, i):
                    kci = kc[i % 2][:, :]
                    w = c.at(f"{pref}_A_{i - 1}")
                    if w:
                        c.scalar.wait_ge(dv.h, w)
                    (c.scalar.activation(out=gsv[:, :], in_=kci, func=AF.Abs)
                     .wge(ldSr[i % 2], c.at(f"{pref}_kld_{i}")))
                    c.scalar.activation(out=gsv[:, :], in_=gsv[:, :],
                                        func=AF.Exp, scale=-1.0)
                    c.scalar.activation(out=gsv[:, :], in_=gsv[:, :],
                                        func=AF.Ln, bias=cst[:, 1:2])
                    c.mark(f"{pref}_gsln_{i}", ac)
                    (c.vector.tensor_scalar(tmpk[:, :], kci, 0.0, None,
                                            ALU.min)
                     .wge(ldSr[i % 2], c.at(f"{pref}_kld_{i}")))
                    (c.vector.tensor_tensor(gsv[:, :], tmpk[:, :], gsv[:, :],
                                            ALU.subtract)
                     .wge(ac, c.at(f"{pref}_gsln_{i}")))
                    c.mark(f"{pref}_kfreeD_{i}", dv)
                    for g in range(KVH):
                        sl = slice(g * CHT, (g + 1) * CHT)
                        c.vector.tensor_tensor_scan(
                            Av[:, sl], gsv[:, sl], gsv[:, sl], 0.0,
                            ALU.add, ALU.bypass)
                    lastc = Av[:, :].rearrange("p (g t) -> p g t", t=CHT
                                               )[:, :, CHT - 1:CHT]
                    c.vector.tensor_tensor(
                        aT_acc[:, :].rearrange("p (g o) -> p g o", o=1),
                        aT_acc[:, :].rearrange("p (g o) -> p g o", o=1),
                        lastc, ALU.add)
                    c.mark(f"{pref}_A_{i}", dv)
                    (c.scalar.activation(
                        out=eCd[:, :].rearrange("p (g o) -> p g o", o=1),
                        in_=lastc, func=AF.Exp, scale=INV_GN)
                     .wge(dv, c.at(f"{pref}_A_{i}")))
                    c.scalar.activation(out=ekv[:, :], in_=kci, func=AF.Exp)
                    c.mark(f"{pref}_kfreeA_{i}", ac)
                    c.scalar.activation(out=emA[:, :], in_=Av[:, :],
                                        func=AF.Exp, scale=-INV_GN)
                    c.mark(f"{pref}_ekv_{i}", ac)

                def z_mm_unit(c, zc, srcs, dst_off, wname, psZmm):
                    n = zc[0]
                    w = c.at(f"z_ev_{n - 1}")
                    if w:
                        c.tensor.wait_ge(dv.h, w)
                    insm = None
                    for j, src in enumerate(srcs):
                        insm = c.tensor.matmul(
                            psZmm[0:1, j * 128:(j + 1) * 128],
                            ones_col[:, :], src, start=True, stop=True)
                        insm.wge(ac, c.at(wname))
                        insm.inc(pe)
                    c.mark(f"z_mm_{n}", pe)
                    (c.vector.tensor_copy(
                        zrow[0:1, dst_off:dst_off + len(srcs) * 128],
                        psZmm[0:1, 0:len(srcs) * 128])
                     .wge(pe, c.at(f"z_mm_{n}")))
                    c.mark(f"z_ev_{n}", dv)
                    zc[0] += 1

                def z_bc_unit(c, zb, src_off, width, recip_mark, psZbc):
                    m = zb[0]
                    w = c.at(f"z_bcev_{m - 1}")
                    if w:
                        c.tensor.wait_ge(ac.h, w)
                    insm = c.tensor.matmul(
                        psZbc[:, 0:width], ones_row[0:1, :],
                        zrow[0:1, src_off:src_off + width],
                        start=True, stop=True)
                    insm.wge(dv, c.at(recip_mark))
                    insm.inc(pe)
                    c.mark(f"z_bc_{m}", pe)
                    w = c.at(f"z_bcused_{m - 2}")
                    if w:
                        c.scalar.wait_ge(dv.h, w)
                    (c.scalar.copy(zbc[m % 2][:, 0:width], psZbc[:, 0:width])
                     .wge(pe, c.at(f"z_bc_{m}")))
                    c.mark(f"z_bcev_{m}", ac)
                    zb[0] += 1
                    return m

                def khat_make(c, pref, i, zb, recip_mark, psZbc):
                    (c.vector.tensor_tensor(ktl[:, :], ekv[:, :], emA[:, :],
                                            ALU.mult)
                     .wge(ac, c.at(f"{pref}_ekv_{i}")))
                    for half in range(2):
                        m = z_bc_unit(c, zb, DM + half * 512, 512,
                                      f"{pref}_zrecip_{i}", psZbc)
                        (c.vector.tensor_tensor(
                            ktl[:, half * 512:(half + 1) * 512],
                            ktl[:, half * 512:(half + 1) * 512],
                            zbc[m % 2][:, 0:512], ALU.mult)
                         .wge(ac, c.at(f"z_bcev_{m}")))
                        c.mark(f"z_bcused_{m}", dv)
                    c.mark(f"{pref}_kt_{i}", dv)
                    for g in range(KVH):
                        sl = slice(g * CHT, (g + 1) * CHT)
                        c.vector.tensor_scalar(khat[:, sl], ktl[:, sl],
                                               eCd[:, g:g + 1], None,
                                               ALU.mult)
                    c.mark(f"{pref}_khat_{i}", dv)

                def khat_transpose(c, pref, i, gTc, psTb):
                    for g in range(KVH):
                        n = gTc[0]
                        base = 256 + (n % 2) * 128
                        w = c.at(f"kh_ev_{n - 2}")
                        if w:
                            c.tensor.wait_ge(dv.h, w)
                        ins = c.tensor.transpose(
                            psTb[:, base:base + 128],
                            khat[:, g * 128:(g + 1) * 128], ident_bf[:, :])
                        ins.wge(dv, c.at(f"{pref}_khat_{i}"))
                        ins.inc(pe)
                        c.mark(f"kh_tr_{n}", pe)
                        (c.vector.tensor_copy(
                            khat_t[:, g * 128:(g + 1) * 128],
                            psTb[:, base:base + 128])
                         .wge(pe, c.at(f"kh_tr_{n}")))
                        c.mark(f"kh_ev_{n}", dv)
                        gTc[0] += 1
                    c.mark(f"{pref}_khatt_{i}", dv)

                def state_update(c, pref, i, ps_a, ps_b, extra_pe_wait=0,
                                 extra_ac_wait=0):
                    w = c.at(f"{pref}_spsfree_{i - 1}")
                    if w:
                        c.tensor.wait_ge(dv.h, w)
                    if extra_ac_wait:
                        c.tensor.wait_ge(ac.h, extra_ac_wait)
                    vsem = [sems["ldV"], sems["ldV1"]][i % 2]
                    c.tensor.wait_ge(vsem.h, c.at(f"{pref}_vld_{i}"))
                    insm = None
                    for g in range(KVH):
                        bank = ps_a if g < 4 else ps_b
                        bsl = slice((g % 4) * 128, (g % 4 + 1) * 128)
                        sl = slice(g * CHT, (g + 1) * CHT)
                        insm = c.tensor.matmul(
                            bank[:, bsl], khat_t[:, sl], vc[i % 2][:, sl],
                            start=True, stop=True)
                        if g == 0:
                            insm.wge(dv, c.at(f"{pref}_khatt_{i}"))
                        insm.inc(pe)
                    c.mark(f"{pref}_smm_{i}", pe)
                    for g in range(KVH):
                        bank = ps_a if g < 4 else ps_b
                        bsl = slice((g % 4) * 128, (g % 4 + 1) * 128)
                        sl = slice(g * CHT, (g + 1) * CHT)
                        ins = c.vector.scalar_tensor_tensor(
                            S_f[:, sl], S_f[:, sl], eCd[:, g:g + 1],
                            bank[:, bsl], ALU.mult, ALU.add)
                        if g == 0:
                            ins.wge(pe, c.at(f"{pref}_smm_{i}"))
                    c.mark(f"{pref}_spsfree_{i}", dv)
                    c.vector.tensor_copy(S_bf[:, :], S_f[:, :])
                    c.mark(f"{pref}_sbf_{i}", dv)

                if "Bpre" in phases:
                    with contextlib.ExitStack() as ps_:
                        Ep = ps_.enter_context
                        psZmm = Ep(nc.psum_tensor("bp_psZm", [128, 512], F32))
                        psZbc = Ep(nc.psum_tensor("bp_psZb", [128, 512], F32))
                        psTb = Ep(nc.psum_tensor("bp_psT", [128, 512], BF16))
                        psSa = Ep(nc.psum_tensor("bp_psSa", [128, 512], F32))
                        psSb = Ep(nc.psum_tensor("bp_psSb", [128, 512], F32))

                        def bpre(c):
                            ebar(c, sems)
                            zc, zb, gTc = [0], [0], [0]
                            for i in range(1, NCHH):
                                kv_load(c, "bp", i,
                                        c.at(f"bp_kfreeD_{i - 2}"),
                                        c.at(f"bp_kfreeA_{i - 2}"),
                                        c.at(f"bp_smm_{i - 2}"))
                                k_gate(c, "bp", i)
                                for half in range(2):
                                    z_mm_unit(
                                        c, zc,
                                        [ekv[:, (half * 4 + g) * 128:
                                             (half * 4 + g + 1) * 128]
                                         for g in range(4)],
                                        DM + half * 512,
                                        f"bp_ekv_{i}", psZmm)
                                (c.vector.reciprocal(zrow[0:1, DM:DM + DKV],
                                                     zrow[0:1, DM:DM + DKV]))
                                c.mark(f"bp_zrecip_{i}", dv)
                                khat_make(c, "bp", i, zb,
                                          f"bp_zrecip_{i}", psZbc)
                                khat_transpose(c, "bp", i, gTc, psTb)
                                state_update(c, "bp", i, psSa, psSb)
                            for g in range(KVH):
                                (c.sync.dma_start(
                                    out=st_loc[g * 128:(g + 1) * 128, :],
                                    in_=S_f[:, g * 128:(g + 1) * 128])
                                 .wge(dv, c.at(f"bp_spsfree_{NCHH - 1}"))
                                 .inc(stSr[g % 2], 16))
                            c.mark("bp_sst0", stSr[0])
                            c.mark("bp_sst1", stSr[1])
                            w = c.at(f"bp_spsfree_{NCHH - 1}")
                            c.tensor.wait_ge(dv.h, w)
                            ins = c.tensor.transpose(
                                psSa[0:KVH, 0:128], aT_acc[:, :],
                                ident_f[:, :])
                            ins.inc(pe)
                            c.mark("bp_att", pe)
                            (c.vector.tensor_copy(tmpk[0:KVH, 0:128],
                                                  psSa[0:KVH, 0:128])
                             .wge(pe, c.at("bp_att")))
                            c.mark("bp_attev", dv)
                            (c.sync.dma_start(out=st_loc[KVH * HD:, :],
                                              in_=tmpk[0:KVH, 0:128])
                             .wge(dv, c.at("bp_attev"))
                             .inc(stSr[0], 16))
                            c.gpsimd.wait_ge(stSr[0].h, stSr[0].n)
                            c.gpsimd.wait_ge(stSr[1].h, stSr[1].n)
                            (c.gpsimd.collective_compute(
                                "AllGather", ALU.bypass,
                                replica_groups=[list(range(NCORES))],
                                ins=[st_loc[:, :].opt()],
                                outs=[st_g[:, :].opt()])
                             .inc(cc))
                            c.mark("bp_cc", cc)
                        run_phase(ctx, block, bpre)

                    with contextlib.ExitStack() as ps_:
                        Ep = ps_.enter_context
                        psW = Ep(nc.psum_tensor("cb_psW", [128, 512], F32))
                        aTA = Ep(nc.sbuf_tensor("cb_aTA", [8, DKV], F32))
                        em_sb = Ep(nc.sbuf_tensor("cb_em", [8, 8], F32))
                        pm_sb = Ep(nc.sbuf_tensor("cb_pm", [128, 8], F32))
                        wv_sb = Ep(nc.sbuf_tensor("cb_wv", [128, 64], F32))
                        sjb = [Ep(nc.sbuf_tensor(f"cb_sj{i}", [128, DKV],
                                                 F32)) for i in range(2)]

                        def comb(c):
                            ebar(c, sems)
                            c.sync.wait_ge(cc.h, cc.n)
                            rowsz = KVH * HD + KVH
                            c.sync.dma_start(out=em_sb[:, :],
                                             in_=emask[:, :]).inc(ldP, 16)
                            c.sync.dma_start(out=pm_sb[:, :],
                                             in_=pmask_t[:, :]).inc(ldP, 16)
                            for r in range(NCORES):
                                (c.sync.dma_start(
                                    out=aTA[r:r + 1, :],
                                    in_=st_g[r * rowsz + KVH * HD:
                                             (r + 1) * rowsz, :].rearrange(
                                        "(o a) b -> o (a b)", o=1))
                                 .inc(ldP, 16))
                            c.mark("cb_ld", ldP)
                            insm = None
                            for g in range(KVH):
                                insm = c.tensor.matmul(
                                    psW[:, g * 8:(g + 1) * 8],
                                    aTA[:, g * 128:(g + 1) * 128],
                                    em_sb[:, :], start=True, stop=True)
                                insm.wge(ldP, c.at("cb_ld"))
                                insm.inc(pe)
                            c.mark("cb_wmm", pe)
                            (c.scalar.activation(out=wv_sb[:, :],
                                                 in_=psW[:, 0:64],
                                                 func=AF.Exp, scale=INV_GN)
                             .wge(pe, c.at("cb_wmm")))
                            c.mark("cb_wexp", ac)
                            (c.vector.tensor_tensor(
                                wv_sb[:, :].rearrange("p (g j) -> p g j", j=8),
                                wv_sb[:, :].rearrange("p (g j) -> p g j", j=8),
                                pm_sb[:, :].rearrange("p (o j) -> p o j", o=1
                                                      ).broadcast_to(
                                    [128, KVH, 8]), ALU.mult)
                             .wge(ac, c.at("cb_wexp")))
                            c.vector.memset(S_f[:, :], 0.0)
                            c.mark("cb_wv", dv)
                            for j in range(NCORES):
                                (c.sync.dma_start(
                                    out=sjb[j % 2][:, :].rearrange(
                                        "p (g d) -> p g d", d=HD),
                                    in_=st_g[j * rowsz:j * rowsz + KVH * HD,
                                             :].rearrange(
                                        "(g p) d -> p g d", p=128))
                                 .wge(dv, c.at(f"cb_sjfree_{j - 2}"))
                                 .inc(ldSr[j % 2], 16))
                                c.mark(f"cb_sjld_{j}", ldSr[j % 2])
                                for g in range(KVH):
                                    sl = slice(g * 128, (g + 1) * 128)
                                    ins = c.vector.scalar_tensor_tensor(
                                        S_f[:, sl], sjb[j % 2][:, sl],
                                        wv_sb[:, g * 8 + j:g * 8 + j + 1],
                                        S_f[:, sl], ALU.mult, ALU.add)
                                    if g == 0:
                                        ins.wge(ldSr[j % 2],
                                                c.at(f"cb_sjld_{j}"))
                                c.mark(f"cb_sjfree_{j}", dv)
                            c.vector.tensor_copy(S_bf[:, :], S_f[:, :])
                            c.mark("cb_done", dv)
                        run_phase(ctx, block, comb)

                if "B" in phases:
                    with contextlib.ExitStack() as ps_:
                        Ep = ps_.enter_context
                        qc = [Ep(nc.sbuf_tensor(f"b_qc{i}", [128, DM], BF16))
                              for i in range(2)]
                        qa = Ep(nc.sbuf_tensor("b_qa", [128, DM], F32))
                        eqv = Ep(nc.sbuf_tensor("b_eq", [128, DM], BF16))
                        eql = Ep(nc.sbuf_tensor("b_eql", [128, DM], BF16))
                        qtl = Ep(nc.sbuf_tensor("b_qt", [128, DM], BF16))
                        qro = Ep(nc.sbuf_tensor("b_qro", [128, DM], BF16))
                        tmpq = Ep(nc.sbuf_tensor("b_tmpq", [128, DM], F32))
                        ptg = Ep(nc.sbuf_tensor("b_ptg", [128, 512], BF16))
                        pts = Ep(nc.sbuf_tensor("b_pts", [128, 512], BF16))
                        prow = Ep(nc.sbuf_tensor("b_prow", [128, 256], BF16))
                        mx = Ep(nc.sbuf_tensor("b_mx", [128, 4], F32))
                        ost = [Ep(nc.sbuf_tensor(f"b_ost{i}", [128, DM],
                                                 BF16)) for i in range(2)]
                        psc = Ep(nc.psum_tensor("bm_psc", [128, 512], F32))
                        psPT = [Ep(nc.psum_tensor(f"bm_psPT{i}", [128, 512],
                                                  F32)) for i in range(2)]
                        psO = [Ep(nc.psum_tensor(f"bm_psO{i}", [128, 512],
                                                 F32)) for i in range(2)]
                        psZmm = Ep(nc.psum_tensor("bm_psZm", [128, 512], F32))
                        psZbc = Ep(nc.psum_tensor("bm_psZb", [128, 512], F32))
                        psTb = Ep(nc.psum_tensor("bm_psT", [128, 512], BF16))

                        def swa_head(c, i, g, hh, zb_unused):
                            h = g * 4 + hh
                            half = hh % 2
                            w = c.at(f"bm_pexp_{i}_{h - 2}")
                            if w:
                                c.tensor.wait_ge(ac.h, w)
                            for kk in range(2):
                                krx = krr[(i - 1 + kk) % 2]
                                insm = c.tensor.matmul(
                                    psc[:, half * 256 + kk * 128:
                                        half * 256 + (kk + 1) * 128],
                                    qro[:, h * 128:(h + 1) * 128],
                                    krx[:, g * 128:(g + 1) * 128],
                                    start=True, stop=True)
                                if kk == 0:
                                    insm.wge(dv, c.at(f"bm_qrope_{i}"))
                                insm.inc(pe)
                            c.mark(f"bm_scmm_{i}_{h}", pe)
                            scs = psc[:, half * 256:(half + 1) * 256]
                            (c.vector.reduce_max(mx[:, half:half + 1], scs,
                                                 axis=AX.X, negate=True)
                             .wge(pe, c.at(f"bm_scmm_{i}_{h}")))
                            c.vector.tensor_scalar(mx[:, half:half + 1],
                                                   mx[:, half:half + 1],
                                                   INV_SQRT_HD, None,
                                                   ALU.mult)
                            c.mark(f"bm_mx_{i}_{h}", dv)
                            mask = mf_sb if i == 1 else mr_sb
                            (c.scalar.activation(
                                out=prow[:, :], in_=scs, func=AF.Exp,
                                scale=INV_SQRT_HD,
                                bias=mx[:, half:half + 1])
                             .wge(dv, c.at(f"bm_mx_{i}_{h}")))
                            c.mark(f"bm_pexp_{i}_{h}", ac)
                            (c.vector.tensor_tensor(prow[:, :], prow[:, :],
                                                    mask[:, :], ALU.mult)
                             .wge(ac, c.at(f"bm_pexp_{i}_{h}")))
                            c.vector.reduce_sum(mx[:, 2:3], prow[:, :],
                                                axis=AX.X)
                            c.vector.reciprocal(mx[:, 2:3], mx[:, 2:3])
                            c.vector.tensor_scalar(prow[:, :], prow[:, :],
                                                   mx[:, 2:3], None, ALU.mult)
                            c.mark(f"bm_pn_{i}_{h}", dv)
                            w = c.at(f"bm_ptsu_{i}_{h - 2}")
                            if w:
                                c.tensor.wait_ge(pe.h, w)
                            for kk in range(2):
                                ins = c.tensor.transpose(
                                    psTb[:, kk * 128:(kk + 1) * 128],
                                    prow[:, kk * 128:(kk + 1) * 128],
                                    ident_bf[:, :])
                                ins.wge(dv, c.at(f"bm_pn_{i}_{h}"))
                                ins.inc(pe)
                            c.mark(f"bm_ptr_{i}_{h}", pe)
                            w = c.at(f"bm_omm_{i}_{h - 2}")
                            if w:
                                c.vector.wait_ge(pe.h, w)
                            (c.vector.tensor_copy(
                                pts[:, half * 256:(half + 1) * 256],
                                psTb[:, 0:256])
                             .wge(pe, c.at(f"bm_ptr_{i}_{h}")))
                            c.mark(f"bm_pts_{i}_{h}", dv)
                            bank = psO[g % 2]
                            ob = bank[:, hh * 128:(hh + 1) * 128]
                            vprev = vc[(i - 1) % 2]
                            vcur = vc[i % 2]
                            sl = slice(g * 128, (g + 1) * 128)
                            if hh == 0:
                                w = c.at(f"bm_oevg_{i}_{g - 2}")
                                if g <= 1:
                                    w = max(w, c.at(f"bm_oevg_{i - 1}_{g + 6}"))
                                if w:
                                    c.tensor.wait_ge(ac.h, w)
                                if g <= 1:
                                    w2 = c.at(f"bm_spsfree_{i - 1}")
                                    if w2:
                                        c.tensor.wait_ge(dv.h, w2)
                                if g == 0:
                                    vsp = [sems["ldV"], sems["ldV1"]]
                                    c.tensor.wait_ge(
                                        vsp[(i - 1) % 2].h,
                                        c.at(f"bm_vld_{i - 1}"))
                                    c.tensor.wait_ge(
                                        vsp[i % 2].h, c.at(f"bm_vld_{i}"))
                            m1 = c.tensor.matmul(
                                ob, vprev[:, sl],
                                pts[:, half * 256:half * 256 + 128],
                                start=True, stop=False)
                            m1.wge(dv, c.at(f"bm_pts_{i}_{h}"))
                            c.tensor.matmul(
                                ob, vcur[:, sl],
                                pts[:, half * 256 + 128:half * 256 + 256],
                                start=False, stop=False)
                            m3 = c.tensor.matmul(
                                ob, vcur[:, sl],
                                ptg[:, hh * 128:(hh + 1) * 128],
                                start=False, stop=False)
                            m3.wge(dv, c.at(f"bm_ptg_{i}_{g}"))
                            m4 = c.tensor.matmul(
                                ob, S_bf[:, sl],
                                qtl[:, h * 128:(h + 1) * 128],
                                start=False, stop=True)
                            m4.wge(dv, c.at(f"bm_sbf_{i - 1}")
                                   if i > 1 else c.at("cb_done"))
                            m4.inc(pe)
                            c.mark(f"bm_omm_{i}_{h}", pe)
                            c.mark(f"bm_ptsu_{i}_{h}", pe)

                        def bmain(c):
                            ebar(c, sems)
                            c.sync.dma_start(out=cosb[:, :],
                                             in_=cos_t[:, :]).inc(ldP, 16)
                            c.sync.dma_start(out=sinb[:, :],
                                             in_=sin_t[:, :]).inc(ldP, 16)
                            ldM = sems["ldM"]
                            (c.gpsimd.dma_start(out=mf_sb[:, :],
                                                 in_=m_first[:, :])
                             .wge(ldM, ldM.n).inc(ldM, 16))
                            (c.gpsimd.dma_start(out=mr_sb[:, :],
                                                in_=m_rest[:, :])
                             .wge(ldM, ldM.n).inc(ldM, 16))
                            (c.gpsimd.dma_start(out=tri_sb[:, :],
                                                in_=tri_m[:, :])
                             .wge(ldM, ldM.n).inc(ldM, 16))
                            c.mark("bm_tbl", ldM)
                            for e in ENGINES:
                                getattr(c, e).wait_ge(ldP.h, ldP.n)
                                getattr(c, e).wait_ge(ldM.h, ldM.n)
                            zc, zb, gTc = [0], [0], [0]
                            kv_load(c, "bm", 0, 0, 0, 0)
                            k_rope(c, "bm", 0)
                            for i in range(1, NCHH):
                                ksem = ldSr[i % 2]
                                vsem = ldGr[i % 2]
                                qsem = ldQr[i % 2]
                                vwait = c.at(f"bm_omm_{i - 1}_{NHEAD - 1}")
                                kv_load(c, "bm", i,
                                        c.at(f"bm_kfreeD_{i - 2}"),
                                        c.at(f"bm_kfreeA_{i - 2}"), vwait)
                                tr = slice(i * CHT, (i + 1) * CHT)
                                if c.at(f"bm_qfreeA_{i - 2}"):
                                    c.sync.wait_ge(ac.h,
                                                   c.at(f"bm_qfreeA_{i - 2}"))
                                (c.sync.dma_start(
                                    out=qc[i % 2][:, :].rearrange(
                                        "p (h t) -> p h t", t=CHT),
                                    in_=q_sp[:, :, tr].rearrange(
                                        "h d t -> d h t"))
                                 .wge(dv, c.at(f"bm_qfreeD_{i - 2}"))
                                 .inc(qsem, 16))
                                c.mark(f"bm_qld_{i}", qsem)
                                k_rope(c, "bm", i)
                                k_gate(c, "bm", i)
                                qci = qc[i % 2][:, :]
                                (c.vector.tensor_scalar(
                                    tmpq[0:64, :], qci[64:128, :], -1.0,
                                    None, ALU.mult)
                                 .wge(qsem, c.at(f"bm_qld_{i}")))
                                c.vector.tensor_scalar(tmpq[64:128, :],
                                                       qci[0:64, :], 1.0,
                                                       None, ALU.mult)
                                q3 = qci.rearrange("p (h t) -> p h t", t=CHT)
                                t3 = tmpq[:, :].rearrange("p (h t) -> p h t",
                                                          t=CHT)
                                cs3 = cosb[:, tr].rearrange(
                                    "p (o t) -> p o t", o=1).broadcast_to(
                                    [128, NHEAD, CHT])
                                sn3 = sinb[:, tr].rearrange(
                                    "p (o t) -> p o t", o=1).broadcast_to(
                                    [128, NHEAD, CHT])
                                qr3 = qro[:, :].rearrange("p (h t) -> p h t",
                                                          t=CHT)
                                c.vector.tensor_tensor(t3, t3, sn3, ALU.mult)
                                c.vector.tensor_tensor(qr3, q3, cs3, ALU.mult)
                                c.vector.tensor_tensor(qr3, qr3, t3, ALU.add)
                                c.mark(f"bm_qrope_{i}", dv)
                                wA = c.at(f"bm_A_{i}")
                                c.vector.wait_ge(dv.h, wA)
                                for h in range(NHEAD):
                                    g = h // GRP
                                    c.vector.scalar_tensor_tensor(
                                        qa[:, h * 128:(h + 1) * 128],
                                        Av[:, g * 128:(g + 1) * 128],
                                        INV_GN,
                                        qci[:, h * 128:(h + 1) * 128],
                                        ALU.mult, ALU.add)
                                c.mark(f"bm_qfreeD_{i}", dv)
                                (c.scalar.activation(out=eql[:, :],
                                                     in_=qa[:, :],
                                                     func=AF.Exp)
                                 .wge(dv, c.at(f"bm_qfreeD_{i}")))
                                c.scalar.activation(out=eqv[:, :], in_=qci,
                                                    func=AF.Exp)
                                c.mark(f"bm_qfreeA_{i}", ac)
                                c.mark(f"bm_eq_{i}", ac)
                                for half in range(2):
                                    z_mm_unit(
                                        c, zc,
                                        [ekv[:, (half * 4 + g) * 128:
                                             (half * 4 + g + 1) * 128]
                                         for g in range(4)],
                                        DM + half * 512,
                                        f"bm_ekv_{i}", psZmm)
                                for r8 in range(8):
                                    z_mm_unit(
                                        c, zc,
                                        [eqv[:, (r8 * 4 + j) * 128:
                                             (r8 * 4 + j + 1) * 128]
                                         for j in range(4)],
                                        r8 * 512, f"bm_eq_{i}", psZmm)
                                c.vector.reciprocal(zrow[0:1, :],
                                                    zrow[0:1, :])
                                c.mark(f"bm_zrecip_{i}", dv)
                                khat_make(c, "bm", i, zb,
                                          f"bm_zrecip_{i}", psZbc)
                                for r8 in range(8):
                                    m = z_bc_unit(c, zb, r8 * 512, 512,
                                                  f"bm_zrecip_{i}", psZbc)
                                    (c.vector.tensor_tensor(
                                        qtl[:, r8 * 512:(r8 + 1) * 512],
                                        eql[:, r8 * 512:(r8 + 1) * 512],
                                        zbc[m % 2][:, 0:512], ALU.mult)
                                     .wge(ac, c.at(f"z_bcev_{m}")))
                                    c.mark(f"z_bcused_{m}", dv)
                                c.mark(f"bm_qtldone_{i}", dv)
                                for g in range(KVH):
                                    bankPT = psPT[g % 2]
                                    w = c.at(f"bm_ptgused_{i}_{g - 2}")
                                    if w:
                                        c.tensor.wait_ge(dv.h, w)
                                    insm = None
                                    for hh in range(4):
                                        h = g * 4 + hh
                                        insm = c.tensor.matmul(
                                            bankPT[:, hh * 128:(hh + 1) * 128],
                                            ktl[:, g * 128:(g + 1) * 128],
                                            qtl[:, h * 128:(h + 1) * 128],
                                            start=True, stop=True)
                                        if hh == 0:
                                            insm.wge(dv,
                                                     c.at(f"bm_qtldone_{i}"))
                                        insm.inc(pe)
                                    c.mark(f"bm_ptgmm_{i}_{g}", pe)
                                    (c.vector.tensor_tensor(
                                        ptg[:, :].rearrange(
                                            "p (h t) -> p h t", t=CHT),
                                        bankPT[:, :].rearrange(
                                            "p (h t) -> p h t", t=CHT),
                                        tri_sb[:, :].rearrange(
                                            "p (o t) -> p o t",
                                            o=1).broadcast_to([128, 4, CHT]),
                                        ALU.mult)
                                     .wge(pe, c.at(f"bm_ptgmm_{i}_{g}")))
                                    c.mark(f"bm_ptg_{i}_{g}", dv)
                                    c.mark(f"bm_ptgused_{i}_{g}", dv)
                                    for hh in range(4):
                                        swa_head(c, i, g, hh, zb)
                                    u = i
                                    if g == 0:
                                        w = c.at(f"bm_ost_{i - 2}")
                                        if w:
                                            c.scalar.wait_ge(
                                                stSr[i % 2].h, w)
                                    (c.scalar.copy(
                                        ost[u % 2][:, g * 512:(g + 1) * 512],
                                        psO[g % 2][:, 0:512])
                                     .wge(pe,
                                          c.at(f"bm_omm_{i}_{g * 4 + 3}")))
                                    c.mark(f"bm_oevg_{i}_{g}", ac)
                                c.mark(f"bm_oev_{i}", ac)
                                khat_transpose(c, "bm", i, gTc, psTb)
                                state_update(c, "bm", i, psO[0], psO[1],
                                             extra_ac_wait=c.at(
                                                 f"bm_oevg_{i}_{KVH - 1}"))
                                c.mark(f"bm_kfree2_{i}", dv)
                                osem = stSr[i % 2]
                                w = c.at(f"bm_ost_{i - 2}")
                                if w:
                                    c.sync.wait_ge(osem.h, w)
                                (c.sync.dma_start(
                                    out=o_sp[:, :, (i - 1) * CHT:i * CHT]
                                    .rearrange("h d t -> d h t"),
                                    in_=ost[i % 2][:, :].rearrange(
                                        "p (h t) -> p h t", t=CHT))
                                 .wge(ac, c.at(f"bm_oev_{i}"))
                                 .inc(osem, 16))
                                c.mark(f"bm_ost_{i}", osem)
                        run_phase(ctx, block, bmain)

        if "B2" in phases:
            with contextlib.ExitStack() as sc_:
                Eb = sc_.enter_context
                ofm = Eb(nc.sbuf_tensor("b2_ofm", [128, 32 * 512], BF16))
                wosl = [Eb(nc.sbuf_tensor(f"b2_wo{i}", [128, 2048], BF16))
                        for i in range(2)]
                xch2 = Eb(nc.sbuf_tensor("b2_x", [128, DM], F32))
                att2 = Eb(nc.sbuf_tensor("b2_att", [128, 2 * DM], F32))
                bn6b = Eb(nc.sbuf_tensor("b2_bn6", [128, 48], F32))
                mvb = Eb(nc.sbuf_tensor("b2_mv", [128, 2], F32))
                rstdb = Eb(nc.sbuf_tensor("b2_rstd", [128, 1], F32))
                x1b = Eb(nc.sbuf_tensor("b2_x1", [128, DM], F32))
                psM = [Eb(nc.psum_tensor(f"b2_ps{i}", [128, 2048], F32))
                       for i in range(2)]
                ldQr2 = [sems["ldQ"], sems["ldQ1"]]

                def b2(c):
                    ebar(c, sems)
                    wu = [0]
                    xu = [0]
                    su = [0]
                    for b in range(4):
                        osem = ldSr[b % 2]
                        (c.sync.dma_start(
                            out=ofm[:, :].rearrange("p (h t) -> p h t",
                                                    t=512),
                            in_=o_sp[:, :, b * 512:(b + 1) * 512]
                            .rearrange("h d t -> d h t"))
                         .wge(pe, c.at(f"b2_mmdone_{b - 2}"))
                         .inc(osem, 16))
                        c.mark(f"b2_old_{b}", osem)
                        for tcg in range(2):
                            for fh in range(2):
                                w = c.at(f"b2_evp_{b}_{tcg}_{fh - 1}") or                                     c.at(f"b2_evp_{b}_{tcg - 1}_1") or                                     c.at(f"b2_evp_{b - 1}_1_1")
                                if w:
                                    c.tensor.wait_ge(ac.h, w)
                                for hdc in range(32):
                                    u = wu[0]
                                    slab = wosl[u % 2]
                                    wsem = ldGr[u % 2]
                                    (c.gpsimd.dma_start(
                                        out=slab[:, :],
                                        in_=wo_bfd[hdc * 128:(hdc + 1) * 128,
                                                   fh * 2048:(fh + 1) * 2048])
                                     .wge(pe, c.at(f"b2_wuse_{u - 2}"))
                                     .inc(wsem, 16))
                                    c.mark(f"b2_wld_{u}", wsem)
                                    if hdc == 0:
                                        c.tensor.wait_ge(osem.h,
                                                         c.at(f"b2_old_{b}"))
                                    insm = None
                                    for tc in range(2):
                                        tcx = tcg * 2 + tc
                                        for fg in range(4):
                                            insm = c.tensor.matmul(
                                                psM[tc][:, fg * 512:
                                                        (fg + 1) * 512],
                                                ofm[:, hdc * 512 + tcx * 128:
                                                    hdc * 512 +
                                                    (tcx + 1) * 128],
                                                slab[:, fg * 512:
                                                     (fg + 1) * 512],
                                                start=(hdc == 0),
                                                stop=(hdc == 31))
                                            if fg == 0 and tc == 0:
                                                insm.wge(wsem,
                                                         c.at(f"b2_wld_{u}"))
                                    insm.inc(pe)
                                    c.mark(f"b2_wuse_{u}", pe)
                                    wu[0] += 1
                                c.mark(f"b2_mm_{b}_{tcg}_{fh}", pe)
                                w = c.at(f"b2_pairdone_{b}_{tcg - 1}") or \
                                    c.at(f"b2_pairdone_{b - 1}_1")
                                if w:
                                    c.scalar.wait_ge(dv.h, w)
                                for tc in range(2):
                                    (c.scalar.copy(
                                        att2[:, tc * DM + fh * 2048:
                                             tc * DM + (fh + 1) * 2048],
                                        psM[tc][:, :])
                                     .wge(pe, c.at(f"b2_mm_{b}_{tcg}_{fh}")))
                                c.mark(f"b2_evp_{b}_{tcg}_{fh}", ac)
                            for tc in range(2):
                                tcx = tcg * 2 + tc
                                chunk = 1 + b * 4 + tcx
                                xsem = ldQr2[xu[0] % 2]
                                (c.sync.dma_start(
                                    out=xch2[:, :],
                                    in_=xin[chunk * CHT:(chunk + 1) * CHT, :])
                                 .wge(dv, c.at(f"b2_x1d_{xu[0] - 1}"))
                                 .inc(xsem, 16))
                                c.mark(f"b2_xld_{xu[0]}", xsem)
                                asl = att2[:, tc * DM:(tc + 1) * DM]
                                for bb in range(8):
                                    ins = c.vector.bn_stats(
                                        bn6b[:, 6 * bb:6 * bb + 6],
                                        asl[:, 512 * bb:512 * (bb + 1)])
                                    if bb == 0:
                                        ins.wge(ac, c.at(
                                            f"b2_evp_{b}_{tcg}_1"))
                                c.vector.bn_aggr(mvb[:, :], bn6b[:, :])
                                c.mark(f"b2_bn_{b}_{tcx}", dv)
                                (c.scalar.activation(
                                    out=rstdb[:, :], in_=mvb[:, 1:2],
                                    func=AF.Ln, bias=cst[:, 0:1], scale=1.0)
                                 .wge(dv, c.at(f"b2_bn_{b}_{tcx}")))
                                c.scalar.activation(
                                    out=rstdb[:, :], in_=rstdb[:, :],
                                    func=AF.Exp, scale=-0.5)
                                c.mark(f"b2_rstd_{b}_{tcx}", ac)
                                w = c.at(f"b2_x1st_{su[0] - 1}")
                                if w:
                                    c.vector.wait_ge(
                                        stSr[(su[0] - 1) % 2].h, w)
                                (c.vector.tensor_scalar(
                                    x1b[:, :], asl, mvb[:, 0:1],
                                    rstdb[:, :], ALU.subtract, ALU.mult)
                                 .wge(ac, c.at(f"b2_rstd_{b}_{tcx}")))
                                wx = c.at(f"b2_xld_{xu[0]}")
                                c.vector.wait_ge(xsem.h, wx)
                                c.vector.tensor_tensor(
                                    x1b[:, :], x1b[:, :], xch2[:, :],
                                    ALU.add)
                                c.mark(f"b2_x1d_{xu[0]}", dv)
                                xu[0] += 1
                                ssem = stSr[su[0] % 2]
                                w = c.at(f"b2_x1st_{su[0] - 2}")
                                if w:
                                    c.sync.wait_ge(ssem.h, w)
                                (c.sync.dma_start(
                                    out=x1_sp[(chunk - 1) * CHT:
                                              chunk * CHT, :],
                                    in_=x1b[:, :])
                                 .wge(dv, c.at(f"b2_x1d_{xu[0] - 1}"))
                                 .inc(ssem, 16))
                                c.mark(f"b2_x1st_{su[0]}", ssem)
                                su[0] += 1
                            c.mark(f"b2_pairdone_{b}_{tcg}", dv)
                        c.mark(f"b2_mmdone_{b}", pe)
                run_phase(ctx, block, b2)

        if "deq" in phases and cfg.get("deq_late"):
            emit_deq()

        if "C" in phases:
            with contextlib.ExitStack() as sc_:
                Ec = sc_.enter_context
                hfm = Ec(nc.sbuf_tensor("c_h", [128, HCF * 512], BF16))
                x1c = Ec(nc.sbuf_tensor("c_x1", [128, DM], F32))
                bn6c = Ec(nc.sbuf_tensor("c_bn6", [128, 48], F32))
                mvc = Ec(nc.sbuf_tensor("c_mv", [128, 2], F32))
                rstdc = Ec(nc.sbuf_tensor("c_rstd", [128, 1], F32))
                outb = Ec(nc.sbuf_tensor("c_out", [128, DM], F32))
                ldQr3 = [sems["ldQ"], sems["ldQ1"]]

                def cwait(c):
                    ebar(c, sems)
                    c.sync.wait_ge(cc.h, cc.n)
                    c.gpsimd.wait_ge(cc.h, cc.n)
                    c.tensor.wait_ge(cc.h, cc.n)
                run_phase(ctx, block, cwait)

                dq12v = dq12_g.rearrange("(r w dc p) h -> r w p dc h",
                                         w=2, dc=32, p=128)
                dq3v = dq3_g.rearrange("(r hc p) d -> r p hc d",
                                       hc=HCF, p=128)

                for b in range(4):
                    with contextlib.ExitStack() as ps_:
                        Ep = ps_.enter_context
                        ln2tm = Ep(nc.sbuf_tensor(f"c1_lntm{b}", [128, DM],
                                                  BF16))
                        ln2fm = Ep(nc.sbuf_tensor(f"c1_lnfm{b}",
                                                  [128, 32 * 512], BF16))
                        w12 = [Ep(nc.sbuf_tensor(f"c1_w{b}_{i}",
                                                 [128, 32 * 128], BF16))
                               for i in range(4)]
                        silub = Ep(nc.sbuf_tensor(f"c1_si{b}", [128, 512],
                                                  BF16))
                        psTc = Ep(nc.psum_tensor(f"c1_psT{b}", [128, 512],
                                                 BF16))
                        psG = [Ep(nc.psum_tensor(f"c1_psG{b}_{i}",
                                                 [128, 512], F32))
                               for i in range(4)]

                        def c1(c, b=b, ln2tm=ln2tm, ln2fm=ln2fm, w12=w12,
                               silub=silub, psTc=psTc, psG=psG):
                            ebar(c, sems)
                            lfm2 = ln2fm[:, :].rearrange(
                                "p (dc t) -> p dc t", t=512)
                            gT = [0]
                            for tc in range(4):
                                chunk = b * 4 + tc
                                xsem = ldQr3[tc % 2]
                                (c.sync.dma_start(
                                    out=x1c[:, :],
                                    in_=x1_sp[chunk * CHT:(chunk + 1) * CHT,
                                              :])
                                 .wge(dv, c.at(f"c1_lntm_{b}_{tc - 1}"))
                                 .inc(xsem, 16))
                                c.mark(f"c1_xld_{b}_{tc}", xsem)
                                for bb in range(8):
                                    ins = c.vector.bn_stats(
                                        bn6c[:, 6 * bb:6 * bb + 6],
                                        x1c[:, 512 * bb:512 * (bb + 1)])
                                    if bb == 0:
                                        ins.wge(xsem,
                                                c.at(f"c1_xld_{b}_{tc}"))
                                c.vector.bn_aggr(mvc[:, :], bn6c[:, :])
                                c.mark(f"c1_bn_{b}_{tc}", dv)
                                (c.scalar.activation(
                                    out=rstdc[:, :], in_=mvc[:, 1:2],
                                    func=AF.Ln, bias=cst[:, 0:1], scale=1.0)
                                 .wge(dv, c.at(f"c1_bn_{b}_{tc}")))
                                c.scalar.activation(
                                    out=rstdc[:, :], in_=rstdc[:, :],
                                    func=AF.Exp, scale=-0.5)
                                c.mark(f"c1_rstd_{b}_{tc}", ac)
                                w = c.at(f"c1_tr_{b}_{tc - 1}")
                                if w:
                                    c.vector.wait_ge(pe.h, w)
                                (c.vector.tensor_scalar(
                                    ln2tm[:, :], x1c[:, :], mvc[:, 0:1],
                                    rstdc[:, :], ALU.subtract, ALU.mult)
                                 .wge(ac, c.at(f"c1_rstd_{b}_{tc}")))
                                c.mark(f"c1_lntm_{b}_{tc}", dv)
                                for g in range(8):
                                    n = gT[0]
                                    w = c.at(f"c1_evT_{b}_{n - 1}")
                                    if w:
                                        c.tensor.wait_ge(dv.h, w)
                                    ins = None
                                    for k in range(4):
                                        dcix = g * 4 + k
                                        ins = c.tensor.transpose(
                                            psTc[:, k * 128:(k + 1) * 128],
                                            ln2tm[:, dcix * 128:
                                                  (dcix + 1) * 128],
                                            ident_bf[:, :])
                                        if k == 0:
                                            ins.wge(dv,
                                                    c.at(f"c1_lntm_{b}_{tc}"))
                                        ins.inc(pe)
                                    c.mark(f"c1_trT_{b}_{n}", pe)
                                    (c.vector.tensor_copy(
                                        lfm2[:, g * 4:g * 4 + 4,
                                             tc * 128:(tc + 1) * 128],
                                        psTc[:, :].rearrange(
                                            "p (k t) -> p k t", t=128))
                                     .wge(pe, c.at(f"c1_trT_{b}_{n}")))
                                    c.mark(f"c1_evT_{b}_{n}", dv)
                                    gT[0] += 1
                                c.mark(f"c1_tr_{b}_{tc}", pe)
                            c.mark(f"c1_lnfm_{b}", dv)
                            for hc in range(HCF):
                                rank, sub = hc // 11, hc % 11
                                for w_i in range(2):
                                    slab = w12[(hc % 2) * 2 + w_i]
                                    wsem = ldSr[hc % 2] if w_i == 0 \
                                        else [sems["ldV"],
                                              sems["ldV1"]][hc % 2]
                                    (c.sync.dma_start(
                                        out=slab[:, :].rearrange(
                                            "p (dc h) -> p dc h", h=128),
                                        in_=dq12v[rank, w_i, :, :,
                                                  sub * 128:(sub + 1) * 128])
                                     .wge(pe, c.at(f"c1_gu_{b}_{hc - 2}"))
                                     .inc(wsem, 16))
                                    c.mark(f"c1_wld_{b}_{hc}_{w_i}", wsem)
                                w = c.at(f"c1_hferm_{b}_{hc - 2}")
                                if w:
                                    c.tensor.wait_ge(dv.h, w)
                                if hc == 0:
                                    c.tensor.wait_ge(dv.h,
                                                     c.at(f"c1_lnfm_{b}"))
                                insm = None
                                for w_i in range(2):
                                    slab = w12[(hc % 2) * 2 + w_i]
                                    wsem = ldSr[hc % 2] if w_i == 0 \
                                        else [sems["ldV"],
                                              sems["ldV1"]][hc % 2]
                                    bank = psG[(hc % 2) * 2 + w_i]
                                    for dc in range(32):
                                        insm = c.tensor.matmul(
                                            bank[:, :],
                                            slab[:, dc * 128:(dc + 1) * 128],
                                            lfm2[:, dc, :],
                                            start=(dc == 0), stop=(dc == 31))
                                        if dc == 0:
                                            insm.wge(
                                                wsem,
                                                c.at(f"c1_wld_{b}_{hc}_{w_i}"))
                                    insm.inc(pe)
                                c.mark(f"c1_gu_{b}_{hc}", pe)
                                w = c.at(f"c1_hferm_{b}_{hc - 1}")
                                if w:
                                    c.scalar.wait_ge(dv.h, w)
                                (c.scalar.activation(
                                    out=silub[:, :],
                                    in_=psG[(hc % 2) * 2][:, :],
                                    func=AF.Silu)
                                 .wge(pe, c.at(f"c1_gu_{b}_{hc}")))
                                c.mark(f"c1_silu_{b}_{hc}", ac)
                                (c.vector.tensor_tensor(
                                    hfm[:, hc * 512:(hc + 1) * 512],
                                    psG[(hc % 2) * 2 + 1][:, :],
                                    silub[:, :], ALU.mult)
                                 .wge(ac, c.at(f"c1_silu_{b}_{hc}")))
                                c.mark(f"c1_hferm_{b}_{hc}", dv)
                            c.mark(f"c1_done_{b}", dv)
                        run_phase(ctx, block, c1)

                    with contextlib.ExitStack() as ps_:
                        Ep = ps_.enter_context
                        w3sl = [Ep(nc.sbuf_tensor(f"c2_w3{b}_{i}",
                                                  [128, HF], BF16))
                                for i in range(2)]
                        yst = Ep(nc.sbuf_tensor(f"c2_yst{b}", [128, 512],
                                                BF16))
                        ytm = Ep(nc.sbuf_tensor(f"c2_ytm{b}", [128, 4 * DM],
                                                BF16))
                        psD = [Ep(nc.psum_tensor(f"c2_psD{b}_{i}",
                                                 [128, 512], F32))
                               for i in range(2)]
                        psTc2 = Ep(nc.psum_tensor(f"c2_psT{b}", [128, 512],
                                                  BF16))

                        def c2(c, b=b, w3sl=w3sl, yst=yst, ytm=ytm, psD=psD,
                               psTc2=psTc2):
                            ebar(c, sems)
                            for dchunk in range(32):
                                rank = dchunk // 4
                                dloc = (dchunk % 4) * 128
                                slab = w3sl[dchunk % 2]
                                wsem = ldSr[dchunk % 2]
                                (c.sync.dma_start(
                                    out=slab[:, :].rearrange(
                                        "p (hc d) -> p hc d", d=128),
                                    in_=dq3v[rank, :, :,
                                             dloc:dloc + 128])
                                 .wge(pe, c.at(f"c2_dmm_{b}_{dchunk - 2}"))
                                 .inc(wsem, 16))
                                c.mark(f"c2_wld_{b}_{dchunk}", wsem)
                                bank = psD[dchunk % 2]
                                w = c.at(f"c2_ev_{b}_{dchunk - 2}")
                                if w:
                                    c.tensor.wait_ge(ac.h, w)
                                insm = None
                                for hc in range(HCF):
                                    insm = c.tensor.matmul(
                                        bank[:, :],
                                        slab[:, hc * 128:(hc + 1) * 128],
                                        hfm[:, hc * 512:(hc + 1) * 512],
                                        start=(hc == 0), stop=(hc == HCF - 1))
                                    if hc == 0:
                                        insm.wge(wsem,
                                                 c.at(f"c2_wld_{b}_{dchunk}"))
                                insm.inc(pe)
                                c.mark(f"c2_dmm_{b}_{dchunk}", pe)
                                w = c.at(f"c2_ytr_{b}_{dchunk - 1}")
                                if w:
                                    c.scalar.wait_ge(pe.h, w)
                                (c.scalar.copy(yst[:, :], bank[:, :])
                                 .wge(pe, c.at(f"c2_dmm_{b}_{dchunk}")))
                                c.mark(f"c2_ev_{b}_{dchunk}", ac)
                                w = c.at(f"c2_ytev_{b}_{dchunk - 1}")
                                if w:
                                    c.tensor.wait_ge(dv.h, w)
                                ins = None
                                for k in range(4):
                                    ins = c.tensor.transpose(
                                        psTc2[:, k * 128:(k + 1) * 128],
                                        yst[:, k * 128:(k + 1) * 128],
                                        ident_bf[:, :])
                                    if k == 0:
                                        ins.wge(ac,
                                                c.at(f"c2_ev_{b}_{dchunk}"))
                                    ins.inc(pe)
                                c.mark(f"c2_ytr_{b}_{dchunk}", pe)
                                (c.vector.tensor_copy(
                                    ytm[:, :].rearrange(
                                        "p (tc d) -> p tc d", d=DM
                                    )[:, :, dchunk * 128:(dchunk + 1) * 128],
                                    psTc2[:, :].rearrange(
                                        "p (k t) -> p k t", t=128))
                                 .wge(pe, c.at(f"c2_ytr_{b}_{dchunk}")))
                                c.mark(f"c2_ytev_{b}_{dchunk}", dv)
                            for tc in range(4):
                                chunk = b * 4 + tc
                                xsem = ldQr3[tc % 2]
                                (c.sync.dma_start(
                                    out=x1c[:, :],
                                    in_=x1_sp[chunk * CHT:(chunk + 1) * CHT,
                                              :])
                                 .wge(dv, c.at(f"c2_out_{b}_{tc - 1}"))
                                 .inc(xsem, 16))
                                c.mark(f"c2_xld_{b}_{tc}", xsem)
                                ysl = ytm[:, tc * DM:(tc + 1) * DM]
                                for bb in range(8):
                                    ins = c.vector.bn_stats(
                                        bn6c[:, 6 * bb:6 * bb + 6],
                                        ysl[:, 512 * bb:512 * (bb + 1)])
                                    if bb == 0:
                                        ins.wge(dv, c.at(
                                            f"c2_ytev_{b}_31"))
                                c.vector.bn_aggr(mvc[:, :], bn6c[:, :])
                                c.mark(f"c2_bn_{b}_{tc}", dv)
                                (c.scalar.activation(
                                    out=rstdc[:, :], in_=mvc[:, 1:2],
                                    func=AF.Ln, bias=cst[:, 0:1], scale=1.0)
                                 .wge(dv, c.at(f"c2_bn_{b}_{tc}")))
                                c.scalar.activation(
                                    out=rstdc[:, :], in_=rstdc[:, :],
                                    func=AF.Exp, scale=-0.5)
                                c.mark(f"c2_rstd_{b}_{tc}", ac)
                                w = c.at(f"c2_ost_{b}_{tc - 1}") if tc > 0 \
                                    else c.at(f"c2_ost_{b - 1}_3")
                                if w:
                                    c.vector.wait_ge(stSr[(tc - 1) % 2].h, w)
                                (c.vector.tensor_scalar(
                                    outb[:, :], ysl, mvc[:, 0:1],
                                    rstdc[:, :], ALU.subtract, ALU.mult)
                                 .wge(ac, c.at(f"c2_rstd_{b}_{tc}")))
                                wx = c.at(f"c2_xld_{b}_{tc}")
                                c.vector.wait_ge(xsem.h, wx)
                                c.vector.tensor_tensor(
                                    outb[:, :], outb[:, :], x1c[:, :],
                                    ALU.add)
                                c.mark(f"c2_out_{b}_{tc}", dv)
                                (c.sync.dma_start(
                                    out=x_out[chunk * CHT:(chunk + 1) * CHT,
                                              :],
                                    in_=outb[:, :])
                                 .wge(dv, c.at(f"c2_out_{b}_{tc}"))
                                 .inc(stSr[tc % 2], 16))
                                c.mark(f"c2_ost_{b}_{tc}", stSr[tc % 2])
                        run_phase(ctx, block, c2)

        def fin(c):
            for s_ in sems.values():
                if s_.n:
                    c.sync.wait_ge(s_.h, s_.n)
        run_phase(ctx, block, fin)

    return nc



def _stage_inputs(inputs):
    x = np.asarray(inputs["L0_attn_hidden_states"], dtype=np.float32)
    wq = np.ascontiguousarray(np.asarray(inputs["L0_attn_Wq"], np.float32))
    wk = np.ascontiguousarray(np.asarray(inputs["L0_attn_Wk"], np.float32))
    wv = np.ascontiguousarray(np.asarray(inputs["L0_attn_Wv"], np.float32))
    wo = np.ascontiguousarray(np.asarray(inputs["L0_attn_Wo"], np.float32))
    w1 = np.asarray(inputs["L0_ffn_w1"], np.float32)
    w2 = np.asarray(inputs["L0_ffn_w2"], np.float32)
    w3 = np.asarray(inputs["L0_ffn_w3"], np.float32)
    xf = np.ascontiguousarray(x.reshape(B_SZ * SEQ, DM))

    ident = np.eye(128, dtype=np.float32)
    a = np.arange(CHT)[:, None]
    kcol = np.arange(2 * CHT)[None, :]
    rel = (a + CHT) - kcol
    base = ((rel >= 0) & (rel <= 64)).astype(np.float32)
    m_rest = np.ascontiguousarray(base)
    m_first = np.ascontiguousarray(base * (kcol >= CHT).astype(np.float32))
    tri = np.ascontiguousarray(
        (np.arange(CHT)[:, None] <= np.arange(CHT)[None, :]).astype(np.float32))

    half = np.arange(HD // 2, dtype=np.float64)
    inv = 1.0 / THETA ** (2.0 * half / HD)
    inv2 = np.concatenate([inv, inv])

    in_maps = []
    for cid in range(NCORES):
        seq = cid // 4
        sl = cid % 4
        r0 = seq * SEQ + sl * TPC
        if sl == 0:
            xs = np.concatenate([np.zeros((CHT, DM), np.float32),
                                 xf[r0:r0 + TPC]], axis=0)
        else:
            xs = np.ascontiguousarray(xf[r0 - CHT:r0 + TPC])
        pos = np.arange(sl * TPC - CHT, sl * TPC + TPC, dtype=np.float64)
        pos = np.maximum(pos, 0.0)
        fr = inv2[:, None] * pos[None, :]
        cos_t = np.ascontiguousarray(np.cos(fr).astype(np.float32))
        sin_t = np.ascontiguousarray(np.sin(fr).astype(np.float32))
        r1 = cid * W1_SHARD
        w1c = np.zeros((W1_SHARD, DM), np.float32)
        w2c = np.zeros((W1_SHARD, DM), np.float32)
        n = max(0, min(W1_SHARD, HF - r1))
        if n > 0:
            w1c[:n] = w1[r1:r1 + n]
            w2c[:n] = w2[r1:r1 + n]
        w3c = np.ascontiguousarray(w3[cid * W3_SHARD:(cid + 1) * W3_SHARD])
        em = np.zeros((8, 8), np.float32)
        pm = np.zeros((8,), np.float32)
        for j in range(8):
            if j // 4 == seq and j < cid:
                pm[j] = 1.0
                for i in range(8):
                    if i // 4 == seq and j < i < cid:
                        em[i, j] = 1.0
        pmt = np.ascontiguousarray(np.tile(pm[None, :], (128, 1)).astype(np.float32))
        cv = np.zeros((128, 8), np.float32)
        cv[:, 0] = EPS
        cv[:, 1] = 1.0
        in_maps.append(dict(
            cvec=cv,
            xin=np.ascontiguousarray(xs), wq=wq, wk=wk, wv=wv, wo=wo,
            w1s=w1c, w2s=w2c, w3s=w3c,
            cos_t=cos_t, sin_t=sin_t, identf=ident,
            m_first=(m_first if sl == 0 else m_rest), m_rest=m_rest,
            tri_m=tri, emask=em, pmask_t=pmt,
        ))
    return in_maps


DEFAULT_CFG = {"phases": ("deq", "A", "Bpre", "B", "B2", "C"), "dbg": set(),
               "deq_late": True}


def kernel(**inputs):
    cfg = DEFAULT_CFG
    nc = build_program(cfg)
    in_maps = _stage_inputs(inputs)
    res = run_bass_kernel_spmd(nc, in_maps, core_ids=list(range(NCORES)))
    out = np.empty((B_SZ * SEQ, DM), np.float32)
    for cid in range(NCORES):
        seq, sl = cid // 4, cid % 4
        r0 = seq * SEQ + sl * TPC
        out[r0:r0 + TPC] = res.results[cid]["x_out"]
    return out.reshape(B_SZ, SEQ, DM)
